# version 02 (final)
# baseline (speedup 1.0000x reference)
import math
import numpy as np
import ml_dtypes
from contextlib import ExitStack

import concourse.bass as bass
import concourse.mybir as mybir
from concourse.bass_utils import run_bass_kernel_spmd

F32 = mybir.dt.float32
BF16 = mybir.dt.bfloat16
AF = mybir.ActivationFunctionType
ALU = mybir.AluOpType

S, D, DC, T = 2048, 2048, 16, 256
NT, L, NCH = S // T, 128, T // 128
INTER, NIC, I4, NXC = 8192, 64, 4096, 32
CONV, NCC, PROJ, NJB = 4352, 34, 8512, 67
EPS_LN, EPS_MRMS = 1e-5, 1e-6
INV_SQRT_D = 1.0 / math.sqrt(2048.0)
SDP_CONST = 0.5 / math.sqrt(8192.0)
SLOT, NSLOT = 2048, 8
STAG_MAX = (NXC + 2) * T


def _h16_sign(a, b):
    return -1.0 if bin(a & b).count("1") % 2 else 1.0


class CH:
    def __init__(self, ctx, nc, name):
        self.sem = ctx.enter_context(nc.semaphore(name))
        self.n = 0

    def bump(self, k=1):
        self.n += k
        return self.n


class Prog:
    ENGS = ("sync", "gpsimd", "vector", "scalar", "tensor")

    def __init__(self):
        self.q = {e: [] for e in self.ENGS}

    def em(self, eng, fn):
        self.q[eng].append(fn)


def build_nc(stage=3):
    nc = bass.Bass("TRN2", target_bir_lowering=False)
    x_in = nc.dram_tensor("x_dmaj", [D, S], F32, kind="ExternalInput").ap()
    w_in_ = nc.dram_tensor("W_in", [PROJ, D], F32, kind="ExternalInput").ap()
    convw = nc.dram_tensor("conv_weight", [CONV, 4], F32, kind="ExternalInput").ap()
    convb = nc.dram_tensor("conv_bias", [CONV], F32, kind="ExternalInput").ap()
    dtbias = nc.dram_tensor("dt_bias", [64], F32, kind="ExternalInput").ap()
    alog = nc.dram_tensor("A_log", [64], F32, kind="ExternalInput").ap()
    dvec = nc.dram_tensor("D_vec", [64], F32, kind="ExternalInput").ap()
    normw = nc.dram_tensor("norm_weight", [I4], F32, kind="ExternalInput").ap()
    w_out_ = nc.dram_tensor("W_out", [D, I4], F32, kind="ExternalInput").ap()
    su_ug = nc.dram_tensor("SU_ug", [D], F32, kind="ExternalInput").ap()
    su_dp = nc.dram_tensor("SU_dp", [INTER], F32, kind="ExternalInput").ap()
    wsc_ug = nc.dram_tensor("Wscale_ug", [2 * INTER], F32, kind="ExternalInput").ap()
    wsc_dp = nc.dram_tensor("Wscale_dp", [D], F32, kind="ExternalInput").ap()
    w_up_ = nc.dram_tensor("W_up", [INTER, D], F32, kind="ExternalInput").ap()
    w_gate_ = nc.dram_tensor("W_gate", [INTER, D], F32, kind="ExternalInput").ap()
    w_down_ = nc.dram_tensor("W_down", [D, INTER], F32, kind="ExternalInput").ap()
    hconst = nc.dram_tensor("Hconst", [128, 128], BF16, kind="ExternalInput").ap()
    out = nc.dram_tensor("out_dmaj", [D, S], F32, kind="ExternalOutput").ap()

    up_nat = nc.dram_tensor("up_nat", [INTER, D], BF16, kind="Internal").ap()
    gate_nat = nc.dram_tensor("gate_nat", [INTER, D], BF16, kind="Internal").ap()
    down_nat = nc.dram_tensor("down_nat", [D, INTER], BF16, kind="Internal").ap()
    in_nat = nc.dram_tensor("in_nat", [PROJ, D], BF16, kind="Internal").ap()
    out_nat = nc.dram_tensor("out_nat", [D, I4], BF16, kind="Internal").ap()
    c_up_t = nc.dram_tensor("c_up", [NIC, 128, DC * 128], BF16, kind="Internal").ap()
    c_gate_t = nc.dram_tensor("c_gate", [NIC, 128, DC * 128], BF16, kind="Internal").ap()
    c_down_t = nc.dram_tensor("c_down", [DC, 128, NIC * 128], BF16, kind="Internal").ap()
    c_win_t = nc.dram_tensor("c_win", [NJB, 128, DC * 128], BF16, kind="Internal").ap()
    c_wout_t = nc.dram_tensor("c_wout", [DC, 128, NXC * 128], BF16, kind="Internal").ap()
    cumscr = nc.dram_tensor("cumscr", [64, T], F32, kind="Internal").ap()
    cdiag = nc.dram_tensor("cdiag", [NCC, 128, 4 * 128], BF16, kind="Internal").ap()
    decscr = nc.dram_tensor("decscr", [NCH, 64], F32, kind="Internal").ap()

    prog = Prog()
    em = prog.em
    with ExitStack() as ctx:
        sb = lambda n_, s_, d_: ctx.enter_context(nc.sbuf_tensor(n_, s_, d_))
        ps = lambda n_, s_, d_: ctx.enter_context(nc.psum_tensor(n_, s_, d_))

        xres = sb("xres", [128, DC * T], BF16)
        usu = sb("usu", [128, DC * T], BF16)
        vbuf = sb("vbuf", [128, DC * T], BF16)
        t2g = sb("t2g", [128, NIC * T], BF16)
        ring = sb("ring", [128, NSLOT * SLOT], BF16)
        xact = sb("xact", [128, (NXC + 2) * T], BF16)
        state = sb("state", [128, I4], F32)
        state16 = sb("state16", [128, I4], BF16)
        xtm = sb("xtm", [128, I4], BF16)
        mgt = sb("mgt", [128, 2 * 512], BF16)
        cexp = sb("cexp", [128, 2 * 512], BF16)
        ebs = sb("ebs", [128, 2 * 512], BF16)
        mexpf = sb("mexpf", [128, 2 * 512], F32)
        wtm = sb("wtm", [128, 64], F32)
        dtptm = sb("dtptm", [128, 64], F32)
        stmp = sb("stmp", [128, 512], F32)
        sgscr = sb("sgscr", [128, 2 * T], F32)
        xbcscr = sb("xbcscr", [128, 2 * (T + 3)], BF16)
        cbuf = sb("cbuf", [128, NCC * 3], BF16)
        sqscr = sb("sqscr", [128, 2 * T], F32)
        lntmp = sb("lntmp", [128, T], F32)
        btm = sb("btm", [128, 128], BF16)
        gtm = sb("gtm", [128, 128], BF16)
        dtraw = sb("dtraw", [64, T], F32)
        dtp = sb("dtp", [64, T], F32)
        ldaf = sb("ldaf", [64, T], F32)
        cumf = sb("cumf", [64, T], F32)
        wff = sb("wff", [64, 128], F32)
        decf = sb("decf", [64, NCH], F32)
        zero64 = sb("zero64", [64, 128], F32)
        s_upsu = sb("s_upsu", [128, NIC], F32)
        s_gate = sb("s_gate", [128, NIC], F32)
        sdp2 = sb("sdp2", [128, DC], F32)
        nwfm = sb("nwfm", [128, NXC], F32)
        cbfm = sb("cbfm", [128, NCC], F32)
        cwfm = sb("cwfm", [128, NCC * 4], F32)
        sudpfm = sb("sudpfm", [128, NIC], F32)
        wugfm = sb("wugfm", [128, NIC], F32)
        ddcol = sb("ddcol", [128, NXC], F32)
        acol = sb("acol", [64, 1], F32)
        dtbcol = sb("dtbcol", [64, 1], F32)
        onecol64 = sb("onecol64", [64, 1], F32)
        surow = sb("surow", [1, D], F32)
        ones1 = sb("ones1", [1, 128], F32)
        nones1 = sb("nones1", [1, 128], F32)
        masklo = sb("masklo", [1, 128], F32)
        maskhi = sb("maskhi", [1, 128], F32)
        drow = sb("drow", [1, 64], F32)
        abrow = sb("abrow", [1, 2 * T], F32)
        strow = sb("strow", [1, 4 * T], F32)
        cumrow = sb("cumrow", [1, 2 * 512], F32)
        decrow = sb("decrow", [1, 64], F32)
        onesf = sb("onesf", [128, 128], F32)
        onesb = sb("onesb", [128, 1], BF16)
        eyef = sb("eyef", [128, 128], F32)
        eyeb = sb("eyeb", [128, 128], BF16)
        masku = sb("masku", [128, 128], F32)
        hpos = sb("hpos", [128, 128], BF16)
        hneg = sb("hneg", [128, 128], BF16)
        dddiag = sb("dddiag", [128, NXC * 128], BF16)
        diagscr = sb("diagscr", [128, 4 * 128], BF16)
        convwb = sb("convwb", [128, 128], F32)
        epscol = sb("epscol", [1, 2], F32)

        pb = [ps(f"pb{i}", [128, 512], F32) for i in range(7)]
        pbt = ps("pbt", [128, 1024], BF16)

        c_wl = CH(ctx, nc, "c_wl")
        c_wls = [CH(ctx, nc, f"c_wls{i}") for i in range(NSLOT)]
        c_row2 = [CH(ctx, nc, "c_row0"), CH(ctx, nc, "c_row1")]
        c_dec = CH(ctx, nc, "c_dec")
        c_cds = CH(ctx, nc, "c_cds")
        c_x = CH(ctx, nc, "c_x")
        c_gl = CH(ctx, nc, "c_gl")
        c_pe = CH(ctx, nc, "c_pe")
        c_dve = CH(ctx, nc, "c_dve")
        c_act = CH(ctx, nc, "c_act")
        c_pool = CH(ctx, nc, "c_pool")
        c_st = CH(ctx, nc, "c_st")
        c_bnc = CH(ctx, nc, "c_bnc")
        c_cast = CH(ctx, nc, "c_cast")
        c_tp = CH(ctx, nc, "c_tp")
        c_cw = CH(ctx, nc, "c_cw")
        c_wfree = CH(ctx, nc, "c_wfree")

        block = ctx.enter_context(nc.Block())

        COMPUTE = ("vector", "scalar", "tensor", "gpsimd")
        CHANS = {"vector": c_dve, "scalar": c_act, "tensor": c_pe, "gpsimd": c_pool}

        def barrier(extra=()):
            snap = {e: CHANS[e].n for e in COMPUTE}
            ex = [(ch.sem, ch.n) for ch in (c_wfree,) + tuple(extra) if ch.n > 0]
            for e in COMPUTE:
                def _(q, e=e, snap=snap, ex=ex):
                    for oe in COMPUTE:
                        if oe != e and snap[oe] > 0:
                            q.wait_ge(CHANS[oe].sem, snap[oe])
                    for sem, n_ in ex:
                        q.wait_ge(sem, n_)
                em(e, _)

        def sg_sl(c):
            return t2g[:, :].bitcast(F32)[:, c * T:(c + 1) * T]

        def _(g):
            g.memset(onesf[:], 1.0)
            g.memset(onesb[:], 1.0)
            g.memset(zero64[:], 0.0)
            g.memset(onecol64[:], 1.0)
            g.memset(ones1[:], 1.0)
            g.memset(nones1[:], -1.0)
            g.memset(masklo[0:1, 0:64], 1.0)
            g.memset(masklo[0:1, 64:128], 0.0)
            g.memset(maskhi[0:1, 0:64], 0.0)
            g.memset(maskhi[0:1, 64:128], 1.0)
            g.memset(epscol[0:1, 0:1], EPS_LN)
            g.memset(epscol[0:1, 1:2], EPS_MRMS)
            g.memset(cbuf[:], 0.0)
            g.memset(state16[:], 0.0).then_inc(c_pool.sem, 1)
            g.wait_ge(c_pool.sem, 1)
            g.affine_select(eyef[:], onesf[:, :], [[1, 128]], ALU.is_equal, 0.0,
                            channel_multiplier=-1)
            g.affine_select(masku[:], onesf[:, :], [[1, 128]], ALU.is_ge, 0.0,
                            channel_multiplier=-1)
            g.dma_start(out=hpos[:], in_=hconst[:, :]).then_inc(c_gl.sem, 16)
            g.dma_start(out=surow[:], in_=su_ug.unsqueeze(0)).then_inc(c_gl.sem, 16)
            g.dma_start(out=acol[:], in_=alog.unsqueeze(1)).then_inc(c_gl.sem, 16)
            g.dma_start(out=dtbcol[:], in_=dtbias.unsqueeze(1)).then_inc(c_gl.sem, 16)
            g.dma_start(out=cwfm[:, :].rearrange("p (c k) -> p c k", c=NCC),
                        in_=convw.rearrange("(c p) k -> p c k", p=128)).then_inc(c_gl.sem, 16)
            for c in range(NCC):
                g.dma_start(out=cbfm[:, c:c + 1], in_=convb[c * 128:(c + 1) * 128].unsqueeze(1)).then_inc(c_gl.sem, 16)
            for c in range(DC):
                g.dma_start(out=sdp2[:, c:c + 1], in_=wsc_dp[c * 128:(c + 1) * 128].unsqueeze(1)).then_inc(c_gl.sem, 16)
            for c in range(NXC):
                g.dma_start(out=nwfm[:, c:c + 1], in_=normw[c * 128:(c + 1) * 128].unsqueeze(1)).then_inc(c_gl.sem, 16)
            for c in range(NIC):
                g.dma_start(out=sudpfm[:, c:c + 1], in_=su_dp[c * 128:(c + 1) * 128].unsqueeze(1)).then_inc(c_gl.sem, 16)
                g.dma_start(out=wugfm[:, c:c + 1], in_=wsc_ug[c * 128:(c + 1) * 128].unsqueeze(1)).then_inc(c_gl.sem, 16)
                g.dma_start(out=s_gate[:, c:c + 1],
                            in_=wsc_ug[INTER + c * 128:INTER + (c + 1) * 128].unsqueeze(1)).then_inc(c_gl.sem, 16)
            g.dma_start(out=drow[0:1, 0:64], in_=dvec.unsqueeze(0)).then_inc(c_gl.sem, 16)
        em("gpsimd", _)
        c_pool.bump(1)
        GL_SMALL = c_gl.bump(16 * (6 + NCC + DC + NXC + 3 * NIC))

        def _(v):
            v.wait_ge(c_gl.sem, GL_SMALL)
            v.tensor_tensor(s_upsu[:, :], wugfm[:, :], sudpfm[:, :], ALU.mult)
            v.drain()
            v.tensor_scalar(sdp2[:, :], sdp2[:, :], SDP_CONST, None, ALU.mult)
            v.drain()
            v.tensor_scalar(hneg[:, :], hpos[:, :], -1.0, None, ALU.mult)
            v.drain()
            v.tensor_copy(eyeb[:], eyef[:]).then_inc(c_dve.sem, 1)
        em("vector", _)
        c_dve.bump()

        def _(s):
            s.wait_ge(c_gl.sem, GL_SMALL)
            s.activation(acol[:, :], acol[:, :], AF.Exp, bias=0.0, scale=1.0)
            s.drain()
            s.mul(acol[:, :], acol[:, :], -1.0).then_inc(c_act.sem, 1)
        em("scalar", _)
        c_act.bump()
        barrier()

        def _(t):
            ev = drow[0:1, :].rearrange("o (c two) -> o two c", two=2)
            t.matmul(pb[0][:, 0:NXC], masklo[0:1, :], ev[:, 0, :], start=True, stop=False)
            t.matmul(pb[0][:, 0:NXC], maskhi[0:1, :], ev[:, 1, :],
                     start=False, stop=True).then_inc(c_pe.sem, 1)
        em("tensor", _)
        PE_DD = c_pe.bump()

        def _(v):
            v.wait_ge(c_pe.sem, PE_DD)
            v.tensor_copy(ddcol[:, :], pb[0][:, 0:NXC])
            v.drain()
            for c in range(NXC):
                v.tensor_scalar(convwb[:, :], onesf[:, :], ddcol[:, c:c + 1], None, ALU.mult)
                v.drain()
                ins = v.tensor_copy(dddiag[:, c * 128:(c + 1) * 128], convwb[:, :])
                v.drain()
            ins.then_inc(c_dve.sem, 1)
        em("vector", _)
        DVE_DD = c_dve.bump()

        def _(g):
            g.wait_ge(c_dve.sem, DVE_DD)
            for c in range(NXC):
                ins = g.affine_select(dddiag[:, c * 128:(c + 1) * 128],
                                      dddiag[:, c * 128:(c + 1) * 128],
                                      [[1, 128]], ALU.is_equal, 0.0, channel_multiplier=-1)
            ins.then_inc(c_pool.sem, 1)
        em("gpsimd", _)
        c_pool.bump()

        cd_after = {}
        for cc in range(NCC):
            w_st = cd_after[cc - 1] if cc >= 1 else 0

            def _(v, cc=cc, w_st=w_st):
                if w_st:
                    v.wait_ge(c_cds.sem, w_st)
                for k in range(4):
                    v.tensor_scalar(convwb[:, :], onesf[:, :],
                                    cwfm[:, cc * 4 + k:cc * 4 + k + 1], None, ALU.mult)
                    v.drain()
                    ins = v.tensor_copy(diagscr[:, k * 128:(k + 1) * 128], convwb[:, :])
                    v.drain()
                ins.then_inc(c_dve.sem, 1)
            em("vector", _)
            bd = c_dve.bump()

            def _(g, cc=cc, bd=bd, wv=c_pool.n + 1):
                g.wait_ge(c_dve.sem, bd)
                for k in range(4):
                    ins = g.affine_select(diagscr[:, k * 128:(k + 1) * 128],
                                          diagscr[:, k * 128:(k + 1) * 128],
                                          [[1, 128]], ALU.is_equal, 0.0,
                                          channel_multiplier=-1)
                ins.then_inc(c_pool.sem, 1)
                g.wait_ge(c_pool.sem, wv)
                g.wait_ge(c_cds.sem, 16 * cc)
                g.dma_start(out=cdiag[cc, :, :], in_=diagscr[:, :]).then_inc(c_cds.sem, 16)
            em("gpsimd", _)
            c_pool.bump()
            cd_after[cc] = c_cds.bump(16)
        CD_DONE = c_cds.n

        stag = xact

        def emit_cache(src_fp32, nat, cache, rblocks, C, bf_src=None):
            ncc_all = C // 128
            for (r0, nr) in rblocks:
                if bf_src is None:
                    def _(g, r0=r0, nr=nr, src_fp32=src_fp32, nat=nat, pv=c_cast.n):
                        g.wait_ge(c_cast.sem, pv)
                        g.dma_start(out=nat[r0:r0 + nr, :], in_=src_fp32[r0:r0 + nr, :]
                                    ).then_inc(c_cast.sem, 16)
                    em("gpsimd", _)
                    c_cast.bump(16)
                cast_done = c_cast.n
                src_nat = nat if bf_src is None else bf_src
                gsz = max(1, min(ncc_all, STAG_MAX // nr))
                cg0 = 0
                while cg0 < ncc_all:
                    gn = min(gsz, ncc_all - cg0)
                    stag_free = c_cw.n

                    def _(sy, r0=r0, nr=nr, cast_done=cast_done, cg0=cg0, gn=gn,
                          src_nat=src_nat, stag_free=stag_free):
                        sy.wait_ge(c_cast.sem, cast_done)
                        sy.wait_ge(c_cw.sem, stag_free)
                        for i in range(gn):
                            cc = cg0 + i
                            sy.dma_start_transpose(
                                stag[:, i * nr:(i + 1) * nr],
                                src_nat[r0:r0 + nr, cc * 128:(cc + 1) * 128],
                            ).then_inc(c_tp.sem, 16)
                    em("sync", _)
                    tp_done = c_tp.bump(16 * gn)
                    nrb = (nr + 127) // 128

                    def _(sc, r0=r0, nr=nr, tp_done=tp_done, cg0=cg0, gn=gn,
                          nrb=nrb, cache=cache, ncc_all=ncc_all):
                        sc.wait_ge(c_tp.sem, tp_done)
                        for rb in range(nrb):
                            w = min(128, nr - rb * 128)
                            src = stag[:, 0:gn * nr].rearrange(
                                "p (c r) -> p c r", c=gn)[:, :, rb * 128:rb * 128 + w]
                            dst = cache[(r0 // 128) + rb, :, :].rearrange(
                                "p (c r) -> p c r", c=ncc_all)[:, cg0:cg0 + gn, 0:w]
                            sc.dma_start(out=dst, in_=src).then_inc(c_cw.sem, 16)
                    em("scalar", _)
                    c_cw.bump(16 * nrb)
                    cg0 += gn

        def rblocks_of(Rtot):
            o_, r = [], 0
            while r < Rtot:
                n = min(512, Rtot - r)
                o_.append((r, n))
                r += n
            return o_

        for blk in rblocks_of(INTER):
            emit_cache(w_up_, up_nat, c_up_t, [blk], D)
            emit_cache(w_gate_, gate_nat, c_gate_t, [blk], D)
        CW_UG = c_cw.n

        wdb = state[:, :].bitcast(BF16)
        for oc in range(DC):
            stag_free = c_cw.n

            def _(g, oc=oc, stag_free=stag_free, pv=c_cast.n):
                g.wait_ge(c_cw.sem, stag_free)
                g.wait_ge(c_cast.sem, pv)
                g.dma_start(out=stag[:, 0:INTER], in_=w_down_[oc * 128:(oc + 1) * 128, :]
                            ).then_inc(c_cast.sem, 16)
            em("gpsimd", _)
            cast_done = c_cast.bump(16)

            def _(v, cast_done=cast_done):
                v.wait_ge(c_cast.sem, cast_done)
                for s_ in range(6):
                    srcb = stag if s_ % 2 == 0 else wdb
                    dstb = wdb if s_ % 2 == 0 else stag
                    step = 1 << s_
                    sv = srcb[:, 0:INTER].rearrange(
                        "p (a two b j) -> p a two b j", two=2, b=step, j=128)
                    dv_ = dstb[:, 0:INTER].rearrange(
                        "p (a two b j) -> p a two b j", two=2, b=step, j=128)
                    v.tensor_tensor(dv_[:, :, 0, :, :], sv[:, :, 0, :, :],
                                    sv[:, :, 1, :, :], ALU.add)
                    v.drain()
                    ins = v.tensor_tensor(dv_[:, :, 1, :, :], sv[:, :, 0, :, :],
                                          sv[:, :, 1, :, :], ALU.subtract)
                    v.drain()
                ins.then_inc(c_dve.sem, 1)
            em("vector", _)
            bf_done = c_dve.bump()

            def _(g, oc=oc, bf_done=bf_done, pv=c_cast.n):
                g.wait_ge(c_dve.sem, bf_done)
                g.wait_ge(c_cast.sem, pv)
                g.dma_start(out=down_nat[oc * 128:(oc + 1) * 128, :], in_=stag[:, 0:INTER]
                            ).then_inc(c_cast.sem, 16)
            em("gpsimd", _)
            c_cast.bump(16)
        emit_cache(None, None, c_down_t, rblocks_of(D), INTER, bf_src=down_nat)
        CW_DOWN = c_cw.n
        emit_cache(w_in_, in_nat, c_win_t, rblocks_of(PROJ), D)
        CW_WIN = c_cw.n
        emit_cache(w_out_, out_nat, c_wout_t, rblocks_of(D), I4)
        CW_ALL = c_cw.n

        def _(g):
            g.wait_ge(c_cw.sem, CW_DOWN)
            g.memset(state[:], 0.0).then_inc(c_pool.sem, 1)
        em("gpsimd", _)
        c_pool.bump()

        ring_state = {"head": 0, "acq": []}

        def ring_acquire(nslots, loader, prewait=0):
            s0 = ring_state["head"] % NSLOT
            if s0 + nslots > NSLOT:
                ring_state["acq"].append([NSLOT - s0, ("pad", 0)])
                ring_state["head"] += NSLOT - s0
                s0 = 0
            ent = [nslots, None]
            ring_state["acq"].append(ent)
            ring_state["head"] += nslots
            total = sum(a[0] for a in ring_state["acq"])
            need = total - NSLOT
            waits = {}
            if need > 0:
                acc = 0
                for a in ring_state["acq"][:-1]:
                    acc += a[0]
                    if a[1] is None:
                        raise RuntimeError("ring depth exceeded")
                    sem, val = a[1]
                    if sem != "pad" and val > waits.get(sem, 0):
                        waits[sem] = val
                    if acc >= need:
                        break
            wl = list(waits.items())
            ch = c_wls[s0]

            def _(q, s0=s0, wl=wl, prewait=prewait, ch=ch):
                if prewait > 0:
                    q.wait_ge(c_cw.sem, prewait)
                for sem, val in wl:
                    q.wait_ge(sem, val)
                loader(q, s0, ch)
            em("sync", _)
            return s0, ent, ch

        def slot_ap(s0, off, n):
            return ring[:, s0 * SLOT + off: s0 * SLOT + off + n]

        def emit_ln(src_sbuf, dst_sbuf, su):
            pe_after = {}
            for c in range(DC):
                def _(s, c=c, w=(pe_after.get(c - 2, 0) if c >= 2 else 0)):
                    pass
                sqa = sqscr[:, (c % 2) * T:(c % 2) * T + T]
                w_pe = pe_after[c - 2] if c >= 2 else 0

                def _(s, c=c, sqa=sqa, w_pe=w_pe, src_sbuf=src_sbuf):
                    if w_pe:
                        s.wait_ge(c_pe.sem, w_pe)
                    s.activation(sqa, src_sbuf[:, c * T:(c + 1) * T],
                                 AF.Square, bias=0.0, scale=1.0).then_inc(c_act.sem, 1)
                em("scalar", _)
                sq_done = c_act.bump()

                def _(t, c=c, sqa=sqa, sq_done=sq_done, src_sbuf=src_sbuf):
                    t.wait_ge(c_act.sem, sq_done)
                    t.matmul(pb[4][0:1, 0:T], onesb[:, 0:1],
                             src_sbuf[:, c * T:(c + 1) * T],
                             start=(c == 0), stop=False)
                    t.matmul(pb[4][0:1, T:2 * T], onesf[:, 0:1], sqa,
                             start=False, stop=(c == DC - 1)).then_inc(c_pe.sem, 1)
                em("tensor", _)
                pe_after[c] = c_pe.bump()
            stats_done = c_pe.n

            def _(v, stats_done=stats_done):
                v.wait_ge(c_pe.sem, stats_done)
                mu = strow[0:1, 0:T]
                e2 = strow[0:1, T:2 * T]
                var = strow[0:1, 2 * T:3 * T]
                v.tensor_scalar(mu, pb[4][0:1, 0:T], 1.0 / D, None, ALU.mult)
                v.drain()
                v.tensor_scalar(e2, pb[4][0:1, T:2 * T], 1.0 / D, None, ALU.mult)
                v.drain()
                v.tensor_tensor(var, mu, mu, ALU.mult)
                v.drain()
                v.tensor_tensor(var, e2, var, ALU.subtract).then_inc(c_dve.sem, 1)
                v.drain()
            em("vector", _)
            var_done = c_dve.bump()

            def _(s, var_done=var_done):
                s.wait_ge(c_dve.sem, var_done)
                s.activation(strow[0:1, 3 * T:4 * T], strow[0:1, 2 * T:3 * T],
                             AF.Sqrt, bias=epscol[0:1, 0:1], scale=1.0
                             ).then_inc(c_act.sem, 1)
            em("scalar", _)
            sd_done = c_act.bump()

            def _(v, sd_done=sd_done):
                v.wait_ge(c_act.sem, sd_done)
                a = abrow[0:1, 0:T]
                b = abrow[0:1, T:2 * T]
                v.reciprocal(a, strow[0:1, 3 * T:4 * T])
                v.drain()
                v.scalar_tensor_tensor(b, strow[0:1, 0:T], -1.0, a,
                                       ALU.mult, ALU.mult).then_inc(c_dve.sem, 1)
                v.drain()
            em("vector", _)
            ab_done = c_dve.bump()

            apply_after = {}
            for c in range(DC):
                w_dve = ab_done if c == 0 else apply_after[c - 1]

                def _(t, c=c, w_dve=w_dve, su=su):
                    t.wait_ge(c_dve.sem, w_dve)
                    lhs = surow[0:1, c * 128:(c + 1) * 128] if su else ones1[0:1, :]
                    t.matmul(pb[5][:, 0:2 * T], lhs, abrow[0:1, 0:2 * T],
                             start=True, stop=True).then_inc(c_pe.sem, 1)
                em("tensor", _)
                bc_done = c_pe.bump()

                def _(v, c=c, bc_done=bc_done, src_sbuf=src_sbuf, dst_sbuf=dst_sbuf):
                    v.wait_ge(c_pe.sem, bc_done)
                    v.tensor_tensor(lntmp[:, 0:T], src_sbuf[:, c * T:(c + 1) * T],
                                    pb[5][:, 0:T], ALU.mult)
                    v.drain()
                    v.tensor_tensor(dst_sbuf[:, c * T:(c + 1) * T], lntmp[:, 0:T],
                                    pb[5][:, T:2 * T], ALU.add).then_inc(c_dve.sem, 1)
                em("vector", _)
                apply_after[c] = c_dve.bump()
            barrier()

        def emit_fwht():
            evac_after = {}
            for cp in range(DC):
                w_act = evac_after[cp - 2] if cp >= 2 else 0

                def _(t, cp=cp, w_act=w_act):
                    if w_act:
                        t.wait_ge(c_act.sem, w_act)
                    bank = pb[cp % 2]
                    for c in range(DC):
                        hm = hpos if _h16_sign(cp, c) > 0 else hneg
                        ins = t.matmul(bank[:, 0:T], hm[:, :], usu[:, c * T:(c + 1) * T],
                                       start=(c == 0), stop=(c == DC - 1))
                    ins.then_inc(c_pe.sem, 1)
                em("tensor", _)
                mmdone = c_pe.bump()

                def _(s, cp=cp, mmdone=mmdone):
                    s.wait_ge(c_pe.sem, mmdone)
                    s.activation(vbuf[:, cp * T:(cp + 1) * T], pb[cp % 2][:, 0:T],
                                 AF.Copy, bias=0.0, scale=INV_SQRT_D
                                 ).then_inc(c_act.sem, 1)
                em("scalar", _)
                evac_after[cp] = c_act.bump()
            barrier()

        def emit_phase_i(first_tile):
            h_after, t2_after = {}, {}
            for ic in range(NIC):
                def lu(q, s0, ch, ic=ic):
                    q.dma_start(out=slot_ap(s0, 0, SLOT), in_=c_up_t[ic, :, :]
                                ).then_inc(ch.sem, 16)

                def lg(q, s0, ch, ic=ic):
                    q.dma_start(out=slot_ap(s0, 0, SLOT), in_=c_gate_t[ic, :, :]
                                ).then_inc(ch.sem, 16)
                pw = CW_UG if (first_tile and ic == 0) else 0
                su0, ent_u, ch_u = ring_acquire(1, lu, prewait=pw)
                ld_u = (ch_u.sem, ch_u.bump(16))
                sg0, ent_g, ch_g = ring_acquire(1, lg)
                ld_g = (ch_g.sem, ch_g.bump(16))
                bu, bg, bt = pb[3 * (ic % 2)], pb[1 + 3 * (ic % 2)], pb[2 + 3 * (ic % 2)]
                w_h = h_after[ic - 2] if ic >= 2 else 0
                w_t2 = t2_after[ic - 3] if ic >= 3 else 0

                def _(t, ic=ic, su0=su0, sg0=sg0, ld_u=ld_u, ld_g=ld_g,
                      bu=bu, bg=bg, w_h=w_h, w_t2=w_t2):
                    if w_h:
                        t.wait_ge(c_dve.sem, w_h)
                        t.wait_ge(c_act.sem, w_t2)
                    t.wait_ge(ld_u[0], ld_u[1])
                    for dc in range(DC):
                        ins = t.matmul(bu[:, 0:T], slot_ap(su0, dc * 128, 128),
                                       vbuf[:, dc * T:(dc + 1) * T],
                                       start=(dc == 0), stop=(dc == DC - 1))
                    ins.then_inc(c_wfree.sem, 1)
                    t.wait_ge(ld_g[0], ld_g[1])
                    for dc in range(DC):
                        ins = t.matmul(bg[:, 0:T], slot_ap(sg0, dc * 128, 128),
                                       vbuf[:, dc * T:(dc + 1) * T],
                                       start=(dc == 0), stop=(dc == DC - 1))
                    ins.then_inc(c_wfree.sem, 1)
                em("tensor", _)
                ent_u[1] = (c_wfree.sem, c_wfree.bump())
                ent_g[1] = (c_wfree.sem, c_wfree.bump())
                upgate_done = ent_g[1][1]

                def _(s, ic=ic, bg=bg, upgate_done=upgate_done):
                    s.wait_ge(c_wfree.sem, upgate_done)
                    s.activation(sgscr[:, (ic % 2) * T:(ic % 2) * T + T], bg[:, 0:T],
                                 AF.Silu, bias=0.0, scale=s_gate[:, ic:ic + 1]
                                 ).then_inc(c_act.sem, 1)
                em("scalar", _)
                sg_done = c_act.bump()

                def _(v, ic=ic, bu=bu, sg_done=sg_done):
                    v.wait_ge(c_act.sem, sg_done)
                    v.scalar_tensor_tensor(
                        usu[:, (ic % DC) * T:(ic % DC) * T + T], bu[:, 0:T],
                        s_upsu[:, ic:ic + 1],
                        sgscr[:, (ic % 2) * T:(ic % 2) * T + T],
                        ALU.mult, ALU.mult).then_inc(c_dve.sem, 1)
                em("vector", _)
                h_after[ic] = c_dve.bump()

                def emit_t2(jc):
                    bt2 = pb[2 + 3 * (jc % 2)]
                    h_done = h_after[jc]

                    w_bt = t2_after.get(jc - 2, 0)

                    def _(t, jc=jc, bt2=bt2, h_done=h_done, w_bt=w_bt):
                        t.wait_ge(c_dve.sem, h_done)
                        if w_bt:
                            t.wait_ge(c_act.sem, w_bt)
                        t.matmul(bt2[:, 0:T], hpos[:, :],
                                 usu[:, (jc % DC) * T:(jc % DC) * T + T],
                                 start=True, stop=True).then_inc(c_pe.sem, 1)
                    em("tensor", _)
                    t2mm_done = c_pe.bump()

                    def _(s, jc=jc, bt2=bt2, t2mm_done=t2mm_done):
                        s.wait_ge(c_pe.sem, t2mm_done)
                        s.activation(t2g[:, jc * T:(jc + 1) * T], bt2[:, 0:T],
                                     AF.Copy, bias=0.0, scale=1.0).then_inc(c_act.sem, 1)
                    em("scalar", _)
                    t2_after[jc] = c_act.bump()

                if ic >= 1:
                    emit_t2(ic - 1)
                if ic == NIC - 1:
                    emit_t2(ic)
            barrier()

        def emit_phase_ii(first_tile):
            res_after = {}
            for o in range(DC):
                def ld(q, s0, ch, o=o):
                    q.dma_start(out=slot_ap(s0, 0, 4 * SLOT), in_=c_down_t[o, :, :]
                                ).then_inc(ch.sem, 16)
                pw = CW_DOWN if (first_tile and o == 0) else 0
                s0, ent, ch_ = ring_acquire(4, ld, prewait=pw)
                ldd = (ch_.sem, ch_.bump(16))
                bank = pb[o % 2]
                w_res = res_after[o - 2] if o >= 2 else 0

                def _(t, o=o, s0=s0, ldd=ldd, bank=bank, w_res=w_res):
                    if w_res:
                        t.wait_ge(c_dve.sem, w_res)
                    t.wait_ge(ldd[0], ldd[1])
                    for ic in range(NIC):
                        ins = t.matmul(bank[:, 0:T], slot_ap(s0, ic * 128, 128),
                                       t2g[:, ic * T:(ic + 1) * T],
                                       start=(ic == 0), stop=(ic == NIC - 1))
                    ins.then_inc(c_wfree.sem, 4)
                em("tensor", _)
                ent[1] = (c_wfree.sem, c_wfree.bump(4))
                mmdone = ent[1][1]

                def _(v, o=o, bank=bank, mmdone=mmdone):
                    v.wait_ge(c_wfree.sem, mmdone)
                    v.scalar_tensor_tensor(
                        xres[:, o * T:(o + 1) * T], bank[:, 0:T], sdp2[:, o:o + 1],
                        xres[:, o * T:(o + 1) * T], ALU.mult, ALU.add
                    ).then_inc(c_dve.sem, 1)
                em("vector", _)
                res_after[o] = c_dve.bump()
            barrier()

        def emit_proj(first_tile):
            if first_tile:
                for e in COMPUTE:
                    def _(q, cwa=CW_ALL):
                        q.wait_ge(c_cw.sem, cwa)
                    em(e, _)
            order = [66] + list(range(66))
            evac_after = {}
            conv_mm_after = {}
            conv_act_after = {}
            cdslot = {}

            def scr(cc):
                return xbcscr[:, (cc % 2) * (T + 3):(cc % 2) * (T + 3) + T + 3]

            for idx, jb in enumerate(order):
                def ld(q, s0, ch, jb=jb):
                    if jb == 66:
                        q.dma_start(
                            out=slot_ap(s0, 0, SLOT).rearrange(
                                "p (c r) -> p c r", c=DC)[:, :, 0:64],
                            in_=c_win_t[jb, :, :].rearrange(
                                "p (c r) -> p c r", c=DC)[:, :, 0:64]
                        ).then_inc(ch.sem, 16)
                    else:
                        q.dma_start(out=slot_ap(s0, 0, SLOT), in_=c_win_t[jb, :, :]
                                    ).then_inc(ch.sem, 16)
                pw = CW_WIN if (first_tile and idx == 0) else 0
                s0, ent, ch_ = ring_acquire(1, ld, prewait=pw)
                ldd = (ch_.sem, ch_.bump(16))
                bank = pb[idx % 2]
                w_ev = evac_after[idx - 2] if idx >= 2 else None

                def _(t, jb=jb, s0=s0, ldd=ldd, bank=bank, w_ev=w_ev):
                    if w_ev is not None:
                        t.wait_ge(w_ev[0], w_ev[1])
                    t.wait_ge(ldd[0], ldd[1])
                    M = 64 if jb == 66 else 128
                    for dc in range(DC):
                        ins = t.matmul(bank[0:M, 0:T], slot_ap(s0, dc * 128, M),
                                       usu[:, dc * T:(dc + 1) * T],
                                       start=(dc == 0), stop=(dc == DC - 1))
                    ins.then_inc(c_wfree.sem, 1)
                em("tensor", _)
                ent[1] = (c_wfree.sem, c_wfree.bump())
                mmdone = ("wfree", ent[1][1])

                if jb == 66:
                    def _(s, bank=bank, mmdone=mmdone):
                        s.wait_ge(c_wfree.sem, mmdone[1])
                        s.activation(dtraw[:, :], bank[0:64, 0:T], AF.Copy,
                                     bias=0.0, scale=1.0).then_inc(c_act.sem, 1)
                    em("scalar", _)
                    evac_after[idx] = (c_act.sem, c_act.bump())
                elif jb < 32:
                    def _(s, jb=jb, bank=bank, mmdone=mmdone):
                        s.wait_ge(c_wfree.sem, mmdone[1])
                        s.activation(sg_sl(jb), bank[:, 0:T], AF.Silu,
                                     bias=0.0, scale=1.0).then_inc(c_act.sem, 1)
                    em("scalar", _)
                    evac_after[idx] = (c_act.sem, c_act.bump())
                else:
                    cc = jb - 32
                    if cc % 4 == 0:
                        def ldc(q, s0, ch, cc=cc, cdd=CD_DONE):
                            q.wait_ge(c_cds.sem, cdd)
                            n_ = min(4, NCC - cc)
                            q.dma_start(out=slot_ap(s0, 0, n_ * 512).rearrange(
                                            "p (c r) -> p c r", c=n_),
                                        in_=cdiag[cc:cc + n_, :, :].transpose([1, 0, 2])
                                        ).then_inc(ch.sem, 16)
                        cds0, cdent, cdch = ring_acquire(1, ldc)
                        cdld = (cdch.sem, cdch.bump(16))
                        cdslot.clear()
                        cdslot.update({"s0": cds0, "ent": cdent, "ld": cdld, "base": cc})
                    w_cm = conv_mm_after[cc - 2] if cc >= 2 else 0

                    def _(s, cc=cc, bank=bank, mmdone=mmdone, w_cm=w_cm):
                        s.wait_ge(c_wfree.sem, mmdone[1])
                        if w_cm:
                            s.wait_ge(c_pe.sem, w_cm)
                        s.activation(scr(cc)[:, 3:T + 3], bank[:, 0:T],
                                     AF.Copy, bias=0.0, scale=1.0).then_inc(c_act.sem, 1)
                    em("scalar", _)
                    raw_done = c_act.bump()
                    evac_after[idx] = (c_act.sem, raw_done)

                    def _(v, cc=cc, raw_done=raw_done):
                        v.wait_ge(c_act.sem, raw_done)
                        v.tensor_copy(scr(cc)[:, 0:3], cbuf[:, cc * 3:cc * 3 + 3])
                        v.drain()
                        v.tensor_copy(cbuf[:, cc * 3:cc * 3 + 3], scr(cc)[:, T:T + 3]
                                      ).then_inc(c_dve.sem, 1)
                        v.drain()
                    em("vector", _)
                    carry_done = c_dve.bump()
                    w_ca = conv_act_after[cc - 2] if cc >= 2 else 0
                    cds0_, cdld_, cdbase_ = cdslot["s0"], cdslot["ld"], cdslot["base"]

                    def _(t, cc=cc, carry_done=carry_done, w_ca=w_ca,
                          cds0_=cds0_, cdld_=cdld_, cdbase_=cdbase_):
                        t.wait_ge(c_dve.sem, carry_done)
                        t.wait_ge(cdld_[0], cdld_[1])
                        if w_ca:
                            t.wait_ge(c_act.sem, w_ca)
                        off = (cc - cdbase_) * 512
                        for k in range(4):
                            ins = t.matmul(pb[2 + cc % 2][:, 0:T],
                                           slot_ap(cds0_, off + k * 128, 128),
                                           scr(cc)[:, k:k + T],
                                           start=(k == 0), stop=(k == 3))
                        ins.then_inc(c_pe.sem, 1)
                    em("tensor", _)
                    conv_mm_after[cc] = c_pe.bump()
                    if cc % 4 == 3 or cc == NCC - 1:
                        cdslot["ent"][1] = (c_pe.sem, conv_mm_after[cc])
                    cv_done = conv_mm_after[cc]

                    def _(s, cc=cc, cv_done=cv_done):
                        s.wait_ge(c_pe.sem, cv_done)
                        s.activation(xact[:, cc * T:(cc + 1) * T],
                                     pb[2 + cc % 2][:, 0:T], AF.Silu,
                                     bias=cbfm[:, cc:cc + 1], scale=1.0
                                     ).then_inc(c_act.sem, 1)
                    em("scalar", _)
                    conv_act_after[cc] = c_act.bump()
            barrier()

        def emit_dtpipe():
            def _(s):
                s.activation(ldaf[:, :], dtraw[:, :], AF.Relu, bias=dtbcol[:, :], scale=1.0)
                s.activation(cumf[:, :], dtraw[:, :], AF.Abs, bias=dtbcol[:, :], scale=1.0)
                s.drain()
                s.activation(cumf[:, :], cumf[:, :], AF.Exp, bias=0.0, scale=-1.0)
                s.drain()
                s.activation(cumf[:, :], cumf[:, :], AF.Ln,
                             bias=onecol64[:, :], scale=1.0).then_inc(c_act.sem, 1)
            em("scalar", _)
            sp_done = c_act.bump()

            def _(v, sp_done=sp_done):
                v.wait_ge(c_act.sem, sp_done)
                v.tensor_tensor(dtp[:, :], ldaf[:, :], cumf[:, :], ALU.add)
                v.drain()
                v.tensor_scalar_min(dtp[:, :], dtp[:, :], 1.0e6)
                v.drain()
                v.tensor_scalar(ldaf[:, :], dtp[:, :], acol[:, 0:1], None, ALU.mult)
                v.drain()
                for l_ in range(NCH):
                    v.tensor_tensor_scan(cumf[:, l_ * L:(l_ + 1) * L],
                                         ldaf[:, l_ * L:(l_ + 1) * L],
                                         zero64[:, :], 0.0, ALU.add, ALU.add)
                    v.drain()
                v.tensor_copy(wff[:, 0:1], cumf[:, 0:1]).then_inc(c_dve.sem, 1)
            em("vector", _)
            cum_done = c_dve.bump()

            def _(s, cum_done=cum_done):
                s.wait_ge(c_dve.sem, cum_done)
                for l_ in range(NCH):
                    ins = s.activation(decf[:, l_:l_ + 1],
                                       cumf[:, l_ * L + L - 1:l_ * L + L],
                                       AF.Exp, bias=0.0, scale=1.0)
                ins.then_inc(c_act.sem, 1)
            em("scalar", _)
            dec_done = c_act.bump()

            def _(g, cum_done=cum_done, dec_done=dec_done):
                g.wait_ge(c_dve.sem, cum_done)
                g.dma_start(out=cumscr[:, :], in_=cumf[:, :]).then_inc(c_bnc.sem, 16)
                g.wait_ge(c_act.sem, dec_done)
                for l_ in range(NCH):
                    g.dma_start(out=decscr[l_, :].unsqueeze(1), in_=decf[:, l_:l_ + 1]
                                ).then_inc(c_bnc.sem, 16)
            em("gpsimd", _)
            c_bnc.bump(16 * (1 + NCH))
            barrier(extra=(c_bnc,))

        def emit_ssd_chunk(l_):
            t0 = l_ * L

            def _(g, l_=l_):
                g.dma_start(out=decrow[0:1, :], in_=decscr[l_, :].unsqueeze(0)
                            ).then_inc(c_dec.sem, 16)
            em("gpsimd", _)
            dec_ld = c_dec.bump(16)

            def _(t, t0=t0):
                t.matmul(pb[4][:, 0:128], xact[:, NXC * T + t0:NXC * T + t0 + L],
                         xact[:, (NXC + 1) * T + t0:(NXC + 1) * T + t0 + L],
                         start=True, stop=True)
                t.transpose(pbt[:, 0:128], xact[:, NXC * T + t0:NXC * T + t0 + L],
                            eyeb[:, :]).then_inc(c_pe.sem, 1)
            em("tensor", _)
            gt_done = c_pe.bump()

            def _(v, gt_done=gt_done):
                v.wait_ge(c_pe.sem, gt_done)
                v.tensor_tensor(gtm[:, :], pb[4][:, 0:128], masku[:, :], ALU.mult)
                v.drain()
                v.tensor_copy(btm[:, :], pbt[:, 0:128]).then_inc(c_dve.sem, 1)
            em("vector", _)
            gtm_done = c_dve.bump()

            def _(v, l_=l_):
                v.tensor_scalar(wff[:, :], cumf[:, l_ * L:(l_ + 1) * L], -1.0,
                                None, ALU.mult)
                v.drain()
                v.tensor_scalar(wff[:, :], wff[:, :],
                                cumf[:, l_ * L + L - 1:l_ * L + L], None, ALU.add
                                ).then_inc(c_dve.sem, 1)
                v.drain()
            em("vector", _)
            wf_done = c_dve.bump()

            def _(s, wf_done=wf_done):
                s.wait_ge(c_dve.sem, wf_done)
                s.drain()
                s.activation(wff[:, :], wff[:, :], AF.Exp, bias=0.0, scale=1.0
                             ).then_inc(c_act.sem, 1)
            em("scalar", _)
            wexp_done = c_act.bump()

            def _(t, l_=l_, wexp_done=wexp_done, gtm_done=gtm_done):
                t.wait_ge(c_act.sem, wexp_done)
                t.wait_ge(c_dve.sem, gtm_done)
                t.transpose(pb[6][:, 0:64], wff[:, :], eyef[0:64, 0:64])
                t.transpose(pb[6][:, 64:128], dtp[:, l_ * L:(l_ + 1) * L],
                            eyef[0:64, 0:64]).then_inc(c_pe.sem, 1)
            em("tensor", _)
            wtr_done = c_pe.bump()

            def _(v, wtr_done=wtr_done):
                v.wait_ge(c_pe.sem, wtr_done)
                v.tensor_copy(wtm[:, :], pb[6][:, 0:64])
                v.drain()
                v.tensor_copy(dtptm[:, :], pb[6][:, 64:128]).then_inc(c_dve.sem, 1)
                v.drain()
            em("vector", _)
            wtm_done = c_dve.bump()

            xtr_after = {}
            for grp in range(4):
                w_prev = gtm_done if grp == 0 else xtr_after[grp - 1]

                def _(t, grp=grp, t0=t0, w_prev=w_prev):
                    t.wait_ge(c_dve.sem, w_prev)
                    for j in range(8):
                        c = grp * 8 + j
                        ins = t.transpose(pbt[:, j * 128:(j + 1) * 128],
                                          xact[:, c * T + t0:c * T + t0 + L], eyeb[:, :])
                    ins.then_inc(c_pe.sem, 1)
                em("tensor", _)
                tr_done = c_pe.bump()

                def _(v, grp=grp, tr_done=tr_done):
                    v.wait_ge(c_pe.sem, tr_done)
                    v.tensor_copy(xtm[:, grp * 1024:(grp + 1) * 1024], pbt[:, :]
                                  ).then_inc(c_dve.sem, 1)
                em("vector", _)
                xtr_after[grp] = c_dve.bump()
            xtr_done = c_dve.n

            def _(v, xtr_done=xtr_done, wtm_done=wtm_done):
                v.wait_ge(c_dve.sem, xtr_done)
                v.wait_ge(c_dve.sem, wtm_done)
                drep = dtptm[:, :].unsqueeze(2).broadcast_to([128, 64, 64])
                v.tensor_tensor(xtm[:, :].rearrange("p (h q) -> p h q", h=64),
                                xtm[:, :].rearrange("p (h q) -> p h q", h=64),
                                drep, ALU.mult)
                v.drain()
                wrep = wtm[:, :].unsqueeze(2).broadcast_to([128, 64, 64])
                v.tensor_tensor(
                    vbuf[:, 0:I4].rearrange("p (h q) -> p h q", h=64),
                    xtm[:, :].rearrange("p (h q) -> p h q", h=64), wrep, ALU.mult
                ).then_inc(c_dve.sem, 1)
                v.drain()
            em("vector", _)
            xw_done = c_dve.bump()

            exp_after = {}
            min_after = {}
            grp_pe_after = {}
            yevac_after = {}
            for g4 in range(16):
                w_pe = grp_pe_after[g4 - 2] if g4 >= 2 else 0

                def _(gq, g4=g4, l_=l_, w_pe=w_pe):
                    if w_pe:
                        gq.wait_ge(c_pe.sem, w_pe)
                    src = cumscr[4 * g4:4 * g4 + 4, l_ * L:(l_ + 1) * L]
                    gq.dma_start(out=cumrow[0:1, (g4 % 2) * 512:(g4 % 2) * 512 + 512]
                                 .rearrange("o (h t) -> o h t", h=4),
                                 in_=src.unsqueeze(0)).then_inc(c_row2[g4 % 2].sem, 16)
                em("gpsimd", _)
                row_ld = c_row2[g4 % 2].bump(16)
                w_min = min_after[g4 - 1] if g4 >= 1 else 0
                w_exp = exp_after[g4 - 1] if g4 >= 1 else 0

                def _(t, g4=g4, row_ld=row_ld, xw_done=xw_done, w_exp=w_exp, w_min=w_min):
                    t.wait_ge(c_row2[g4 % 2].sem, row_ld)
                    if g4 == 0:
                        t.wait_ge(c_dve.sem, xw_done)
                    if w_min:
                        t.wait_ge(c_dve.sem, w_min)
                    if w_exp:
                        t.wait_ge(c_act.sem, w_exp)
                    cr = cumrow[0:1, (g4 % 2) * 512:(g4 % 2) * 512 + 512]
                    t.matmul(pb[3][:, 0:512], ones1[0:1, :], cr, start=True, stop=False)
                    for k in range(4):
                        ins = t.matmul(pb[3][:, k * 128:(k + 1) * 128],
                                       cr[0:1, k * 128:(k + 1) * 128], nones1[0:1, :],
                                       start=False, stop=(k == 3))
                    ins.then_inc(c_pe.sem, 1)
                    t.matmul(pb[4][:, 0:512], ones1[0:1, :], cr,
                             start=True, stop=True).then_inc(c_pe.sem, 1)
                em("tensor", _)
                c_pe.bump()
                dmat_done = c_pe.bump()
                w_eslot = exp_after[g4 - 2] if g4 >= 2 else 0

                def _(v, g4=g4, dmat_done=dmat_done, w_eslot=w_eslot):
                    v.wait_ge(c_pe.sem, dmat_done)
                    if w_eslot:
                        v.wait_ge(c_act.sem, w_eslot)
                    v.tensor_scalar_min(mexpf[:, (g4 % 2) * 512:(g4 % 2) * 512 + 512],
                                        pb[3][:, 0:512], 0.0).then_inc(c_dve.sem, 1)
                    v.drain()
                em("vector", _)
                min_after[g4] = c_dve.bump()
                mind = min_after[g4]
                w_slot = grp_pe_after[g4 - 2] if g4 >= 2 else 0

                def _(s, g4=g4, mind=mind, w_slot=w_slot):
                    s.wait_ge(c_dve.sem, mind)
                    if w_slot:
                        s.wait_ge(c_pe.sem, w_slot)
                    s.activation(mgt[:, (g4 % 2) * 512:(g4 % 2) * 512 + 512],
                                 mexpf[:, (g4 % 2) * 512:(g4 % 2) * 512 + 512],
                                 AF.Exp, bias=0.0, scale=1.0)
                    s.activation(ebs[:, (g4 % 2) * 512:(g4 % 2) * 512 + 512],
                                 pb[4][:, 0:512], AF.Exp, bias=0.0, scale=1.0
                                 ).then_inc(c_act.sem, 1)
                em("scalar", _)
                exp_after[g4] = c_act.bump()
                expd = exp_after[g4]

                def _(v, g4=g4, l_=l_, expd=expd):
                    v.wait_ge(c_act.sem, expd)
                    sl_m = mgt[:, (g4 % 2) * 512:(g4 % 2) * 512 + 512]
                    sl_c = cexp[:, (g4 % 2) * 512:(g4 % 2) * 512 + 512]
                    sl_e = ebs[:, (g4 % 2) * 512:(g4 % 2) * 512 + 512]
                    grep = gtm[:, :].unsqueeze(1).broadcast_to([128, 4, 128])
                    v.tensor_tensor(sl_m.rearrange("p (h t) -> p h t", h=4),
                                    sl_m.rearrange("p (h t) -> p h t", h=4),
                                    grep, ALU.mult)
                    v.drain()
                    crep = xact[:, (NXC + 1) * T + l_ * L:(NXC + 1) * T + l_ * L + L] \
                        .unsqueeze(1).broadcast_to([128, 4, 128])
                    v.tensor_tensor(sl_c.rearrange("p (h t) -> p h t", h=4), crep,
                                    sl_e.rearrange("p (h t) -> p h t", h=4), ALU.mult
                                    ).then_inc(c_dve.sem, 1)
                    v.drain()
                em("vector", _)
                combd = c_dve.bump()

                ybank = pb[5 + (g4 % 2)]
                w_ye = yevac_after[g4 - 2] if g4 >= 2 else 0

                def _(t, g4=g4, t0=t0, ybank=ybank, combd=combd, w_ye=w_ye):
                    t.wait_ge(c_dve.sem, combd)
                    if w_ye:
                        t.wait_ge(c_dve.sem, w_ye)
                    sl_m = mgt[:, (g4 % 2) * 512:(g4 % 2) * 512 + 512]
                    sl_c = cexp[:, (g4 % 2) * 512:(g4 % 2) * 512 + 512]
                    mms = [(ybank[:, 0:128], dddiag[:, 2 * g4 * 128:(2 * g4 + 1) * 128],
                            xact[:, 2 * g4 * T + t0:2 * g4 * T + t0 + L])]
                    for k in range(4):
                        h = 4 * g4 + k
                        po = 64 * (h % 2)
                        pj = (k // 2) * 128
                        mms.append((ybank[po:po + 64, pj:pj + 128],
                                    xtm[:, h * 64:h * 64 + 64],
                                    sl_m[:, k * 128:(k + 1) * 128]))
                        mms.append((ybank[po:po + 64, pj:pj + 128],
                                    state16[:, h * 64:h * 64 + 64],
                                    sl_c[:, k * 128:(k + 1) * 128]))
                    c2 = 2 * g4 + 1
                    mms.append((ybank[:, 128:256], dddiag[:, c2 * 128:(c2 + 1) * 128],
                                xact[:, c2 * T + t0:c2 * T + t0 + L]))
                    for i, (o_, l2, r2) in enumerate(mms):
                        ins = t.matmul(o_, l2, r2, start=(i == 0), stop=(i == len(mms) - 1))
                    ins.then_inc(c_pe.sem, 1)
                em("tensor", _)
                grp_pe_after[g4] = c_pe.bump()
                ymm_done = grp_pe_after[g4]

                def _(v, g4=g4, t0=t0, ybank=ybank, ymm_done=ymm_done):
                    v.wait_ge(c_pe.sem, ymm_done)
                    dst = xact[:, :].rearrange("p (c t) -> p c t", c=NXC + 2)[
                        :, 2 * g4:2 * g4 + 2, t0:t0 + L]
                    sg2 = t2g[:, :].bitcast(F32).rearrange("p (c t) -> p c t", c=NXC)[
                        :, 2 * g4:2 * g4 + 2, t0:t0 + L]
                    v.tensor_tensor(dst, ybank[:, 0:256].rearrange("p (c t) -> p c t", c=2),
                                    sg2, ALU.mult).then_inc(c_dve.sem, 1)
                em("vector", _)
                yevac_after[g4] = c_dve.bump()

            scomb_after = {}
            exp_last = exp_after[15]
            for g8 in range(8):
                w_sc = scomb_after[g8 - 1] if g8 >= 1 else 0

                def _(t, g8=g8, dec_ld=dec_ld, w_sc=w_sc, exp_last=exp_last):
                    t.wait_ge(c_dec.sem, dec_ld)
                    if g8 == 0:
                        t.wait_ge(c_act.sem, exp_last)
                    if w_sc:
                        t.wait_ge(c_dve.sem, w_sc)
                    t.matmul(pb[3][:, 0:512], btm[:, :],
                             vbuf[:, g8 * 512:(g8 + 1) * 512],
                             start=True, stop=True).then_inc(c_pe.sem, 1)
                    dsl = decrow[0:1, g8 * 8:(g8 + 1) * 8].unsqueeze(2) \
                        .broadcast_to([1, 8, 64])
                    t.matmul(pb[4][:, 0:512].rearrange("p (h q) -> p h q", h=8),
                             ones1[0:1, :], dsl, start=True, stop=True
                             ).then_inc(c_pe.sem, 1)
                em("tensor", _)
                c_pe.bump()
                smm_done = c_pe.bump()

                def _(v, g8=g8, smm_done=smm_done):
                    v.wait_ge(c_pe.sem, smm_done)
                    v.tensor_tensor(stmp[:, :], state[:, g8 * 512:(g8 + 1) * 512],
                                    pb[4][:, 0:512], ALU.mult)
                    v.drain()
                    v.tensor_tensor(state[:, g8 * 512:(g8 + 1) * 512],
                                    stmp[:, :], pb[3][:, 0:512], ALU.add)
                    v.drain()
                    v.tensor_copy(state16[:, g8 * 512:(g8 + 1) * 512],
                                  state[:, g8 * 512:(g8 + 1) * 512]
                                  ).then_inc(c_dve.sem, 1)
                    v.drain()
                em("vector", _)
                scomb_after[g8] = c_dve.bump()
            barrier()

        def emit_mamba_tail(first_tile):
            pe_after = {}
            for c in range(NXC):
                w_pe = pe_after[c - 2] if c >= 2 else 0

                def _(s, c=c, w_pe=w_pe):
                    if w_pe:
                        s.wait_ge(c_pe.sem, w_pe)
                    s.activation(sqscr[:, (c % 2) * T:(c % 2) * T + T],
                                 xact[:, c * T:(c + 1) * T], AF.Square,
                                 bias=0.0, scale=1.0).then_inc(c_act.sem, 1)
                em("scalar", _)
                sq_done = c_act.bump()

                def _(t, c=c, sq_done=sq_done):
                    t.wait_ge(c_act.sem, sq_done)
                    t.matmul(pb[4][0:1, 0:T], onesf[:, 0:1],
                             sqscr[:, (c % 2) * T:(c % 2) * T + T],
                             start=(c == 0), stop=(c == NXC - 1)).then_inc(c_pe.sem, 1)
                em("tensor", _)
                pe_after[c] = c_pe.bump()
            stats_done = c_pe.n

            def _(v, stats_done=stats_done):
                v.wait_ge(c_pe.sem, stats_done)
                v.tensor_scalar(strow[0:1, 0:T], pb[4][0:1, 0:T], 1.0 / I4,
                                None, ALU.mult).then_inc(c_dve.sem, 1)
                v.drain()
            em("vector", _)
            var_done = c_dve.bump()

            def _(s, var_done=var_done):
                s.wait_ge(c_dve.sem, var_done)
                s.activation(strow[0:1, T:2 * T], strow[0:1, 0:T], AF.Sqrt,
                             bias=epscol[0:1, 1:2], scale=1.0).then_inc(c_act.sem, 1)
            em("scalar", _)
            sd_done = c_act.bump()

            def _(v, sd_done=sd_done):
                v.wait_ge(c_act.sem, sd_done)
                v.reciprocal(strow[0:1, 2 * T:3 * T], strow[0:1, T:2 * T]
                             ).then_inc(c_dve.sem, 1)
                v.drain()
            em("vector", _)
            rstd_done = c_dve.bump()

            def _(t, rstd_done=rstd_done):
                t.wait_ge(c_dve.sem, rstd_done)
                t.matmul(pb[4][:, 0:T], ones1[0:1, :], strow[0:1, 2 * T:3 * T],
                         start=True, stop=True).then_inc(c_pe.sem, 1)
            em("tensor", _)
            rst_done = c_pe.bump()

            for c in range(NXC):
                w0 = rst_done if c == 0 else 0

                def _(v, c=c, w0=w0):
                    if w0:
                        v.wait_ge(c_pe.sem, w0)
                    v.scalar_tensor_tensor(
                        xact[:, c * T:(c + 1) * T], xact[:, c * T:(c + 1) * T],
                        nwfm[:, c:c + 1], pb[4][:, 0:T], ALU.mult, ALU.mult
                    ).then_inc(c_dve.sem, 1)
                em("vector", _)
                c_dve.bump()
            yn_done = c_dve.n

            res_after = {}
            for o in range(DC):
                def ld(q, s0, ch, o=o):
                    q.dma_start(out=slot_ap(s0, 0, 2 * SLOT), in_=c_wout_t[o, :, :]
                                ).then_inc(ch.sem, 16)
                pw = CW_ALL if (first_tile and o == 0) else 0
                s0, ent, ch_ = ring_acquire(2, ld, prewait=pw)
                ldd = (ch_.sem, ch_.bump(16))
                bank = pb[o % 2]
                w0 = yn_done if o == 0 else 0
                w_res = res_after[o - 2] if o >= 2 else 0

                def _(t, o=o, s0=s0, ldd=ldd, bank=bank, w0=w0, w_res=w_res):
                    if w0:
                        t.wait_ge(c_dve.sem, w0)
                    if w_res:
                        t.wait_ge(c_dve.sem, w_res)
                    t.wait_ge(ldd[0], ldd[1])
                    for c in range(NXC):
                        ins = t.matmul(bank[:, 0:T], slot_ap(s0, c * 128, 128),
                                       xact[:, c * T:(c + 1) * T],
                                       start=(c == 0), stop=(c == NXC - 1))
                    ins.then_inc(c_wfree.sem, 2)
                em("tensor", _)
                ent[1] = (c_wfree.sem, c_wfree.bump(2))
                mmdone = ent[1][1]

                def _(v, o=o, bank=bank, mmdone=mmdone):
                    v.wait_ge(c_wfree.sem, mmdone)
                    v.scalar_tensor_tensor(
                        xres[:, o * T:(o + 1) * T], bank[:, 0:T], 1.0,
                        xres[:, o * T:(o + 1) * T], ALU.mult, ALU.add
                    ).then_inc(c_dve.sem, 1)
                em("vector", _)
                res_after[o] = c_dve.bump()
            barrier()

        def emit_store(tile):
            def _(g, tile=tile):
                g.dma_start(out=out.rearrange("(c p) s -> p c s", p=128)[
                    :, :, tile * T:(tile + 1) * T],
                    in_=xres[:, :].rearrange("p (c t) -> p c t", c=DC)
                ).then_inc(c_st.sem, 16)
            em("gpsimd", _)
            c_st.bump(16)

        NT_EFF = 1 if stage >= 40 else NT
        for tile in range(NT_EFF):
            def _(g, tile=tile):
                if tile > 0:
                    g.wait_ge(c_st.sem, tile * 16)
                g.dma_start(out=xres[:, :].rearrange("p (c t) -> p c t", c=DC),
                            in_=x_in.rearrange("(c p) s -> p c s", p=128)[
                                :, :, tile * T:(tile + 1) * T]).then_inc(c_x.sem, 16)
            em("gpsimd", _)
            x_ld = c_x.bump(16)
            for e in COMPUTE:
                def _(q, x_ld=x_ld):
                    q.wait_ge(c_x.sem, x_ld)
                em(e, _)

            emit_ln(xres, usu, su=True)
            emit_fwht()
            emit_phase_i(first_tile=(tile == 0))
            emit_phase_ii(first_tile=(tile == 0))
            if stage == 1:
                emit_store(tile)
                continue

            emit_ln(xres, usu, su=False)
            emit_proj(first_tile=(tile == 0))
            emit_dtpipe()
            if stage == 45:
                def _(g, tile=tile):
                    g.dma_start(out=out.rearrange("(c p) s -> p c s", p=128)[
                        :, :, tile * T:(tile + 1) * T],
                        in_=usu[:, :].rearrange("p (c t) -> p c t", c=DC)
                    ).then_inc(c_st.sem, 16)
                em("gpsimd", _)
                c_st.bump(16)
                def _(g):
                    g.wait_ge(c_cw.sem, CW_ALL)
                    g.wait_ge(c_st.sem, c_st.n)
                em("gpsimd", _)
                break
            if stage in (46, 47):
                def _(g, tile=tile, k=stage - 46):
                    g.dma_start(out=out.rearrange("(c p) s -> p c s", p=128)[
                        :, :, tile * T:(tile + 1) * T],
                        in_=t2g[:, :].bitcast(F32).rearrange(
                            "p (c t) -> p c t", c=NXC)[:, k * 16:(k + 1) * 16, :]
                    ).then_inc(c_st.sem, 16)
                em("gpsimd", _)
                c_st.bump(16)
                def _(g):
                    g.wait_ge(c_cw.sem, CW_ALL)
                    g.wait_ge(c_st.sem, c_st.n)
                em("gpsimd", _)
                break
            if stage in (40, 41):
                def _(g, tile=tile, k=stage - 40):
                    g.dma_start(out=out.rearrange("(c p) s -> p c s", p=128)[
                        :, :, tile * T:(tile + 1) * T],
                        in_=xact[:, :].rearrange("p (c t) -> p c t", c=NXC + 2)[
                            :, k * 16:(k + 1) * 16, :]
                    ).then_inc(c_st.sem, 16)
                em("gpsimd", _)
                c_st.bump(16)
                def _(g):
                    g.wait_ge(c_cw.sem, CW_ALL)
                    g.wait_ge(c_st.sem, c_st.n)
                em("gpsimd", _)
                break
            for l_ in range(NCH):
                emit_ssd_chunk(l_)
            if stage in (42, 43):
                def _(g, tile=tile, k=stage - 42):
                    g.dma_start(out=out.rearrange("(c p) s -> p c s", p=128)[
                        :, :, tile * T:(tile + 1) * T],
                        in_=xact[:, :].rearrange("p (c t) -> p c t", c=NXC + 2)[
                            :, k * 16:(k + 1) * 16, :]
                    ).then_inc(c_st.sem, 16)
                em("gpsimd", _)
                c_st.bump(16)
                def _(g):
                    g.wait_ge(c_cw.sem, CW_ALL)
                    g.wait_ge(c_st.sem, c_st.n)
                em("gpsimd", _)
                break
            emit_mamba_tail(first_tile=(tile == 0))
            if stage == 2:
                emit_store(tile)
                continue

            emit_ln(xres, usu, su=True)
            emit_fwht()
            emit_phase_i(first_tile=False)
            emit_phase_ii(first_tile=False)
            emit_store(tile)

        def _(g, cwa=None):
            g.wait_ge(c_cw.sem, CW_ALL)
            g.wait_ge(c_st.sem, c_st.n)
        em("gpsimd", _)

        @block.sync
        def _(sync):
            for fn in prog.q["sync"]:
                fn(sync)

        @block.gpsimd
        def _(g):
            for fn in prog.q["gpsimd"]:
                fn(g)

        @block.vector
        def _(v):
            for fn in prog.q["vector"]:
                fn(v)

        @block.scalar
        def _(s):
            for fn in prog.q["scalar"]:
                fn(s)

        @block.tensor
        def _(t):
            for fn in prog.q["tensor"]:
                fn(t)

    return nc


_NC_CACHE = {}


def _get_nc(stage=3):
    if stage not in _NC_CACHE:
        _NC_CACHE[stage] = build_nc(stage)
    return _NC_CACHE[stage]


def _hmat():
    idx = np.arange(128)
    pc = np.zeros((128, 128), np.int64)
    for b in range(7):
        pc += ((idx[:, None] >> b) & 1) * ((idx[None, :] >> b) & 1)
    return np.where(pc % 2 == 0, 1.0, -1.0).astype(ml_dtypes.bfloat16)


def kernel(L0_attn_hidden_states, L0_attn_W_in, L0_attn_conv_weight,
           L0_attn_conv_bias, L0_attn_dt_bias, L0_attn_A_log, L0_attn_D,
           L0_attn_norm_weight, L0_attn_W_out, L0_ffn_SU_ug, L0_ffn_SU_dp,
           L0_ffn_Wscale_ug, L0_ffn_Wscale_dp, L0_ffn_W_up, L0_ffn_W_gate,
           L0_ffn_W_down, _stage=3):
    nc = _get_nc(_stage)
    common = {
        "W_in": np.ascontiguousarray(L0_attn_W_in, np.float32),
        "conv_weight": np.ascontiguousarray(L0_attn_conv_weight, np.float32),
        "conv_bias": np.ascontiguousarray(L0_attn_conv_bias, np.float32),
        "dt_bias": np.ascontiguousarray(L0_attn_dt_bias, np.float32),
        "A_log": np.ascontiguousarray(L0_attn_A_log, np.float32),
        "D_vec": np.ascontiguousarray(L0_attn_D, np.float32),
        "norm_weight": np.ascontiguousarray(L0_attn_norm_weight, np.float32),
        "W_out": np.ascontiguousarray(L0_attn_W_out, np.float32),
        "SU_ug": np.ascontiguousarray(L0_ffn_SU_ug, np.float32),
        "SU_dp": np.ascontiguousarray(L0_ffn_SU_dp, np.float32),
        "Wscale_ug": np.ascontiguousarray(L0_ffn_Wscale_ug, np.float32),
        "Wscale_dp": np.ascontiguousarray(L0_ffn_Wscale_dp, np.float32),
        "W_up": np.ascontiguousarray(L0_ffn_W_up, np.float32),
        "W_gate": np.ascontiguousarray(L0_ffn_W_gate, np.float32),
        "W_down": np.ascontiguousarray(L0_ffn_W_down, np.float32),
        "Hconst": _hmat(),
    }
    hs = np.asarray(L0_attn_hidden_states, np.float32)
    in_maps = []
    for core in range(8):
        m = dict(common)
        m["x_dmaj"] = np.ascontiguousarray(hs[core].T)
        in_maps.append(m)
    res = run_bass_kernel_spmd(nc, in_maps, core_ids=list(range(8)))
    outs = [np.ascontiguousarray(r["out_dmaj"].T) for r in res.results]
    return np.stack(outs, axis=0)


if __name__ == "__main__":
    import sys
    nc = build_nc(int(sys.argv[1]) if len(sys.argv) > 1 else 3)
    print("built ok; instructions:", nc.n_instructions())


# version 03
# speedup vs baseline: 1.0081x; 1.0081x over previous
import math
import numpy as np
import ml_dtypes
from contextlib import ExitStack

import concourse.bass as bass
import concourse.mybir as mybir
from concourse.bass_utils import run_bass_kernel_spmd

F32 = mybir.dt.float32
BF16 = mybir.dt.bfloat16
AF = mybir.ActivationFunctionType
ALU = mybir.AluOpType

S, D, DC, T = 2048, 2048, 16, 256
NT, L, NCH = S // T, 128, T // 128
INTER, NIC, I4, NXC = 8192, 64, 4096, 32
CONV, NCC, PROJ, NJB = 4352, 34, 8512, 67
EPS_LN, EPS_MRMS = 1e-5, 1e-6
INV_SQRT_D = 1.0 / math.sqrt(2048.0)
SDP_CONST = 0.5 / math.sqrt(8192.0)
SLOT, NSLOT = 2048, 8
STAG_MAX = (NXC + 2) * T


def _h16_sign(a, b):
    return -1.0 if bin(a & b).count("1") % 2 else 1.0


class CH:
    def __init__(self, ctx, nc, name):
        self.sem = ctx.enter_context(nc.semaphore(name))
        self.n = 0

    def bump(self, k=1):
        self.n += k
        return self.n


class Prog:
    ENGS = ("sync", "gpsimd", "vector", "scalar", "tensor")

    def __init__(self):
        self.q = {e: [] for e in self.ENGS}

    def em(self, eng, fn):
        self.q[eng].append(fn)


def build_nc(stage=3):
    nc = bass.Bass("TRN2", target_bir_lowering=False)
    x_in = nc.dram_tensor("x_dmaj", [D, S], F32, kind="ExternalInput").ap()
    w_in_ = nc.dram_tensor("W_in", [PROJ, D], F32, kind="ExternalInput").ap()
    convw = nc.dram_tensor("conv_weight", [CONV, 4], F32, kind="ExternalInput").ap()
    convb = nc.dram_tensor("conv_bias", [CONV], F32, kind="ExternalInput").ap()
    dtbias = nc.dram_tensor("dt_bias", [64], F32, kind="ExternalInput").ap()
    alog = nc.dram_tensor("A_log", [64], F32, kind="ExternalInput").ap()
    dvec = nc.dram_tensor("D_vec", [64], F32, kind="ExternalInput").ap()
    normw = nc.dram_tensor("norm_weight", [I4], F32, kind="ExternalInput").ap()
    w_out_ = nc.dram_tensor("W_out", [D, I4], F32, kind="ExternalInput").ap()
    su_ug = nc.dram_tensor("SU_ug", [D], F32, kind="ExternalInput").ap()
    su_dp = nc.dram_tensor("SU_dp", [INTER], F32, kind="ExternalInput").ap()
    wsc_ug = nc.dram_tensor("Wscale_ug", [2 * INTER], F32, kind="ExternalInput").ap()
    wsc_dp = nc.dram_tensor("Wscale_dp", [D], F32, kind="ExternalInput").ap()
    w_up_ = nc.dram_tensor("W_up", [INTER, D], F32, kind="ExternalInput").ap()
    w_gate_ = nc.dram_tensor("W_gate", [INTER, D], F32, kind="ExternalInput").ap()
    w_down_ = nc.dram_tensor("W_down", [D, INTER], F32, kind="ExternalInput").ap()
    hconst = nc.dram_tensor("Hconst", [128, 128], BF16, kind="ExternalInput").ap()
    out = nc.dram_tensor("out_dmaj", [D, S], F32, kind="ExternalOutput").ap()

    up_nat = nc.dram_tensor("up_nat", [INTER, D], BF16, kind="Internal").ap()
    gate_nat = nc.dram_tensor("gate_nat", [INTER, D], BF16, kind="Internal").ap()
    down_nat = nc.dram_tensor("down_nat", [D, INTER], BF16, kind="Internal").ap()
    in_nat = nc.dram_tensor("in_nat", [PROJ, D], BF16, kind="Internal").ap()
    out_nat = nc.dram_tensor("out_nat", [D, I4], BF16, kind="Internal").ap()
    c_up_t = nc.dram_tensor("c_up", [NIC, 128, DC * 128], BF16, kind="Internal").ap()
    c_gate_t = nc.dram_tensor("c_gate", [NIC, 128, DC * 128], BF16, kind="Internal").ap()
    c_down_t = nc.dram_tensor("c_down", [DC, 128, NIC * 128], BF16, kind="Internal").ap()
    c_win_t = nc.dram_tensor("c_win", [NJB, 128, DC * 128], BF16, kind="Internal").ap()
    c_wout_t = nc.dram_tensor("c_wout", [DC, 128, NXC * 128], BF16, kind="Internal").ap()
    cumscr = nc.dram_tensor("cumscr", [64, T], F32, kind="Internal").ap()
    cdiag = nc.dram_tensor("cdiag", [NCC, 128, 4 * 128], BF16, kind="Internal").ap()
    decscr = nc.dram_tensor("decscr", [NCH, 64], F32, kind="Internal").ap()

    prog = Prog()
    em = prog.em
    with ExitStack() as ctx:
        sb = lambda n_, s_, d_: ctx.enter_context(nc.sbuf_tensor(n_, s_, d_))
        ps = lambda n_, s_, d_: ctx.enter_context(nc.psum_tensor(n_, s_, d_))

        xres = sb("xres", [128, DC * T], BF16)
        usu = sb("usu", [128, DC * T], BF16)
        vbuf = sb("vbuf", [128, DC * T], BF16)
        t2g = sb("t2g", [128, NIC * T], BF16)
        ring = sb("ring", [128, NSLOT * SLOT], BF16)
        xact = sb("xact", [128, (NXC + 2) * T], BF16)
        state = sb("state", [128, I4], F32)
        state16 = sb("state16", [128, I4], BF16)
        xtm = sb("xtm", [128, I4], BF16)
        mgt = sb("mgt", [128, 2 * 512], BF16)
        cexp = sb("cexp", [128, 2 * 512], BF16)
        ebs = sb("ebs", [128, 2 * 512], BF16)
        mexpf = sb("mexpf", [128, 2 * 512], F32)
        wtm = sb("wtm", [128, 64], F32)
        dtptm = sb("dtptm", [128, 64], F32)
        stmp = sb("stmp", [128, 512], F32)
        sgscr = sb("sgscr", [128, 2 * T], F32)
        xbcscr = sb("xbcscr", [128, 2 * (T + 3)], BF16)
        cbuf = sb("cbuf", [128, NCC * 3], BF16)
        sqscr = sb("sqscr", [128, 2 * T], F32)
        lntmp = sb("lntmp", [128, T], F32)
        btm = sb("btm", [128, 128], BF16)
        gtm = sb("gtm", [128, 128], BF16)
        dtraw = sb("dtraw", [64, T], F32)
        dtp = sb("dtp", [64, T], F32)
        ldaf = sb("ldaf", [64, T], F32)
        cumf = sb("cumf", [64, T], F32)
        wff = sb("wff", [64, 128], F32)
        decf = sb("decf", [64, NCH], F32)
        zero64 = sb("zero64", [64, 128], F32)
        s_upsu = sb("s_upsu", [128, NIC], F32)
        s_gate = sb("s_gate", [128, NIC], F32)
        sdp2 = sb("sdp2", [128, DC], F32)
        nwfm = sb("nwfm", [128, NXC], F32)
        cbfm = sb("cbfm", [128, NCC], F32)
        cwfm = sb("cwfm", [128, NCC * 4], F32)
        sudpfm = sb("sudpfm", [128, NIC], F32)
        wugfm = sb("wugfm", [128, NIC], F32)
        ddcol = sb("ddcol", [128, NXC], F32)
        acol = sb("acol", [64, 1], F32)
        dtbcol = sb("dtbcol", [64, 1], F32)
        onecol64 = sb("onecol64", [64, 1], F32)
        surow = sb("surow", [1, D], F32)
        ones1 = sb("ones1", [1, 128], F32)
        nones1 = sb("nones1", [1, 128], F32)
        masklo = sb("masklo", [1, 128], F32)
        maskhi = sb("maskhi", [1, 128], F32)
        drow = sb("drow", [1, 64], F32)
        abrow = sb("abrow", [1, 2 * T], F32)
        strow = sb("strow", [1, 4 * T], F32)
        cumrow = sb("cumrow", [1, 2 * 512], F32)
        decrow = sb("decrow", [1, 64], F32)
        onesf = sb("onesf", [128, 128], F32)
        onesb = sb("onesb", [128, 1], BF16)
        eyef = sb("eyef", [128, 128], F32)
        eyeb = sb("eyeb", [128, 128], BF16)
        masku = sb("masku", [128, 128], F32)
        hpos = sb("hpos", [128, 128], BF16)
        hneg = sb("hneg", [128, 128], BF16)
        dddiag = sb("dddiag", [128, NXC * 128], BF16)
        diagscr = sb("diagscr", [128, 4 * 128], BF16)
        convwb = sb("convwb", [128, 128], F32)
        epscol = sb("epscol", [1, 2], F32)

        pb = [ps(f"pb{i}", [128, 512], F32) for i in range(7)]
        pbt = ps("pbt", [128, 1024], BF16)

        c_wl = CH(ctx, nc, "c_wl")
        c_wls = [CH(ctx, nc, f"c_wls{i}") for i in range(NSLOT)]
        c_row2 = [CH(ctx, nc, "c_row0"), CH(ctx, nc, "c_row1")]
        c_dec = CH(ctx, nc, "c_dec")
        c_cds = CH(ctx, nc, "c_cds")
        c_cast2 = [CH(ctx, nc, "c_cast2a"), CH(ctx, nc, "c_cast2b")]
        castidx = [0]
        c_x = CH(ctx, nc, "c_x")
        c_gl = CH(ctx, nc, "c_gl")
        c_pe = CH(ctx, nc, "c_pe")
        c_dve = CH(ctx, nc, "c_dve")
        c_act = CH(ctx, nc, "c_act")
        c_pool = CH(ctx, nc, "c_pool")
        c_st = CH(ctx, nc, "c_st")
        c_bnc = CH(ctx, nc, "c_bnc")
        c_cast = CH(ctx, nc, "c_cast")
        c_tp = CH(ctx, nc, "c_tp")
        c_cw = CH(ctx, nc, "c_cw")
        c_wfree = CH(ctx, nc, "c_wfree")

        block = ctx.enter_context(nc.Block())

        COMPUTE = ("vector", "scalar", "tensor", "gpsimd")
        CHANS = {"vector": c_dve, "scalar": c_act, "tensor": c_pe, "gpsimd": c_pool}

        def barrier(extra=()):
            snap = {e: CHANS[e].n for e in COMPUTE}
            ex = [(ch.sem, ch.n) for ch in (c_wfree,) + tuple(extra) if ch.n > 0]
            for e in COMPUTE:
                def _(q, e=e, snap=snap, ex=ex):
                    for oe in COMPUTE:
                        if oe != e and snap[oe] > 0:
                            q.wait_ge(CHANS[oe].sem, snap[oe])
                    for sem, n_ in ex:
                        q.wait_ge(sem, n_)
                em(e, _)

        def sg_sl(c):
            return t2g[:, :].bitcast(F32)[:, c * T:(c + 1) * T]

        def _(g):
            g.memset(onesf[:], 1.0)
            g.memset(onesb[:], 1.0)
            g.memset(zero64[:], 0.0)
            g.memset(onecol64[:], 1.0)
            g.memset(ones1[:], 1.0)
            g.memset(nones1[:], -1.0)
            g.memset(masklo[0:1, 0:64], 1.0)
            g.memset(masklo[0:1, 64:128], 0.0)
            g.memset(maskhi[0:1, 0:64], 0.0)
            g.memset(maskhi[0:1, 64:128], 1.0)
            g.memset(epscol[0:1, 0:1], EPS_LN)
            g.memset(epscol[0:1, 1:2], EPS_MRMS)
            g.memset(cbuf[:], 0.0)
            g.memset(state16[:], 0.0).then_inc(c_pool.sem, 1)
            g.wait_ge(c_pool.sem, 1)
            g.affine_select(eyef[:], onesf[:, :], [[1, 128]], ALU.is_equal, 0.0,
                            channel_multiplier=-1)
            g.affine_select(masku[:], onesf[:, :], [[1, 128]], ALU.is_ge, 0.0,
                            channel_multiplier=-1)
            g.dma_start(out=hpos[:], in_=hconst[:, :]).then_inc(c_gl.sem, 16)
            g.dma_start(out=surow[:], in_=su_ug.unsqueeze(0)).then_inc(c_gl.sem, 16)
            g.dma_start(out=acol[:], in_=alog.unsqueeze(1)).then_inc(c_gl.sem, 16)
            g.dma_start(out=dtbcol[:], in_=dtbias.unsqueeze(1)).then_inc(c_gl.sem, 16)
            g.dma_start(out=cwfm[:, :].rearrange("p (c k) -> p c k", c=NCC),
                        in_=convw.rearrange("(c p) k -> p c k", p=128)).then_inc(c_gl.sem, 16)
            for c in range(NCC):
                g.dma_start(out=cbfm[:, c:c + 1], in_=convb[c * 128:(c + 1) * 128].unsqueeze(1)).then_inc(c_gl.sem, 16)
            for c in range(DC):
                g.dma_start(out=sdp2[:, c:c + 1], in_=wsc_dp[c * 128:(c + 1) * 128].unsqueeze(1)).then_inc(c_gl.sem, 16)
            for c in range(NXC):
                g.dma_start(out=nwfm[:, c:c + 1], in_=normw[c * 128:(c + 1) * 128].unsqueeze(1)).then_inc(c_gl.sem, 16)
            for c in range(NIC):
                g.dma_start(out=sudpfm[:, c:c + 1], in_=su_dp[c * 128:(c + 1) * 128].unsqueeze(1)).then_inc(c_gl.sem, 16)
                g.dma_start(out=wugfm[:, c:c + 1], in_=wsc_ug[c * 128:(c + 1) * 128].unsqueeze(1)).then_inc(c_gl.sem, 16)
                g.dma_start(out=s_gate[:, c:c + 1],
                            in_=wsc_ug[INTER + c * 128:INTER + (c + 1) * 128].unsqueeze(1)).then_inc(c_gl.sem, 16)
            g.dma_start(out=drow[0:1, 0:64], in_=dvec.unsqueeze(0)).then_inc(c_gl.sem, 16)
        em("gpsimd", _)
        c_pool.bump(1)
        GL_SMALL = c_gl.bump(16 * (6 + NCC + DC + NXC + 3 * NIC))

        def _(v):
            v.wait_ge(c_gl.sem, GL_SMALL)
            v.tensor_tensor(s_upsu[:, :], wugfm[:, :], sudpfm[:, :], ALU.mult)
            v.drain()
            v.tensor_scalar(sdp2[:, :], sdp2[:, :], SDP_CONST, None, ALU.mult)
            v.drain()
            v.tensor_scalar(hneg[:, :], hpos[:, :], -1.0, None, ALU.mult)
            v.drain()
            v.tensor_copy(eyeb[:], eyef[:]).then_inc(c_dve.sem, 1)
        em("vector", _)
        c_dve.bump()

        def _(s):
            s.wait_ge(c_gl.sem, GL_SMALL)
            s.activation(acol[:, :], acol[:, :], AF.Exp, bias=0.0, scale=1.0)
            s.drain()
            s.mul(acol[:, :], acol[:, :], -1.0).then_inc(c_act.sem, 1)
        em("scalar", _)
        c_act.bump()
        barrier()

        def _(t):
            ev = drow[0:1, :].rearrange("o (c two) -> o two c", two=2)
            t.matmul(pb[0][:, 0:NXC], masklo[0:1, :], ev[:, 0, :], start=True, stop=False)
            t.matmul(pb[0][:, 0:NXC], maskhi[0:1, :], ev[:, 1, :],
                     start=False, stop=True).then_inc(c_pe.sem, 1)
        em("tensor", _)
        PE_DD = c_pe.bump()

        def _(v):
            v.wait_ge(c_pe.sem, PE_DD)
            v.tensor_copy(ddcol[:, :], pb[0][:, 0:NXC])
            v.drain()
            for c in range(NXC):
                v.tensor_scalar(convwb[:, :], onesf[:, :], ddcol[:, c:c + 1], None, ALU.mult)
                v.drain()
                ins = v.tensor_copy(dddiag[:, c * 128:(c + 1) * 128], convwb[:, :])
                v.drain()
            ins.then_inc(c_dve.sem, 1)
        em("vector", _)
        DVE_DD = c_dve.bump()

        def _(g):
            g.wait_ge(c_dve.sem, DVE_DD)
            for c in range(NXC):
                ins = g.affine_select(dddiag[:, c * 128:(c + 1) * 128],
                                      dddiag[:, c * 128:(c + 1) * 128],
                                      [[1, 128]], ALU.is_equal, 0.0, channel_multiplier=-1)
            ins.then_inc(c_pool.sem, 1)
        em("gpsimd", _)
        c_pool.bump()

        cd_after = {}
        for cc in range(NCC):
            w_st = cd_after[cc - 1] if cc >= 1 else 0

            def _(v, cc=cc, w_st=w_st):
                if w_st:
                    v.wait_ge(c_cds.sem, w_st)
                for k in range(4):
                    v.tensor_scalar(convwb[:, :], onesf[:, :],
                                    cwfm[:, cc * 4 + k:cc * 4 + k + 1], None, ALU.mult)
                    v.drain()
                    ins = v.tensor_copy(diagscr[:, k * 128:(k + 1) * 128], convwb[:, :])
                    v.drain()
                ins.then_inc(c_dve.sem, 1)
            em("vector", _)
            bd = c_dve.bump()

            def _(g, cc=cc, bd=bd, wv=c_pool.n + 1):
                g.wait_ge(c_dve.sem, bd)
                for k in range(4):
                    ins = g.affine_select(diagscr[:, k * 128:(k + 1) * 128],
                                          diagscr[:, k * 128:(k + 1) * 128],
                                          [[1, 128]], ALU.is_equal, 0.0,
                                          channel_multiplier=-1)
                ins.then_inc(c_pool.sem, 1)
                g.wait_ge(c_pool.sem, wv)
                g.wait_ge(c_cds.sem, 16 * cc)
                g.dma_start(out=cdiag[cc, :, :], in_=diagscr[:, :]).then_inc(c_cds.sem, 16)
            em("gpsimd", _)
            c_pool.bump()
            cd_after[cc] = c_cds.bump(16)
        CD_DONE = c_cds.n

        stag = xact

        def emit_cache(src_fp32, nat, cache, rblocks, C, bf_src=None):
            ncc_all = C // 128
            for (r0, nr) in rblocks:
                if bf_src is None:
                    chc = c_cast2[castidx[0] % 2]
                    castidx[0] += 1

                    def _(g, r0=r0, nr=nr, src_fp32=src_fp32, nat=nat,
                          chc=chc, pv=chc.n):
                        g.wait_ge(chc.sem, pv)
                        g.dma_start(out=nat[r0:r0 + nr, :], in_=src_fp32[r0:r0 + nr, :]
                                    ).then_inc(chc.sem, 16)
                    em("gpsimd", _)
                    cast_done = (chc.sem, chc.bump(16))
                else:
                    cast_done = (c_cast.sem, c_cast.n)
                src_nat = nat if bf_src is None else bf_src
                gsz = max(1, min(ncc_all, STAG_MAX // nr))
                cg0 = 0
                while cg0 < ncc_all:
                    gn = min(gsz, ncc_all - cg0)
                    stag_free = c_cw.n

                    def _(sy, r0=r0, nr=nr, cast_done=cast_done, cg0=cg0, gn=gn,
                          src_nat=src_nat, stag_free=stag_free):
                        sy.wait_ge(cast_done[0], cast_done[1])
                        sy.wait_ge(c_cw.sem, stag_free)
                        for i in range(gn):
                            cc = cg0 + i
                            sy.dma_start_transpose(
                                stag[:, i * nr:(i + 1) * nr],
                                src_nat[r0:r0 + nr, cc * 128:(cc + 1) * 128],
                            ).then_inc(c_tp.sem, 16)
                    em("sync", _)
                    tp_done = c_tp.bump(16 * gn)
                    nrb = (nr + 127) // 128

                    def _(sc, r0=r0, nr=nr, tp_done=tp_done, cg0=cg0, gn=gn,
                          nrb=nrb, cache=cache, ncc_all=ncc_all):
                        sc.wait_ge(c_tp.sem, tp_done)
                        for rb in range(nrb):
                            w = min(128, nr - rb * 128)
                            src = stag[:, 0:gn * nr].rearrange(
                                "p (c r) -> p c r", c=gn)[:, :, rb * 128:rb * 128 + w]
                            dst = cache[(r0 // 128) + rb, :, :].rearrange(
                                "p (c r) -> p c r", c=ncc_all)[:, cg0:cg0 + gn, 0:w]
                            sc.dma_start(out=dst, in_=src).then_inc(c_cw.sem, 16)
                    em("scalar", _)
                    c_cw.bump(16 * nrb)
                    cg0 += gn

        def rblocks_of(Rtot):
            o_, r = [], 0
            while r < Rtot:
                n = min(512, Rtot - r)
                o_.append((r, n))
                r += n
            return o_

        for blk in rblocks_of(INTER):
            emit_cache(w_up_, up_nat, c_up_t, [blk], D)
            emit_cache(w_gate_, gate_nat, c_gate_t, [blk], D)
        CW_UG = c_cw.n

        wdb = state[:, :].bitcast(BF16)
        for oc in range(DC):
            stag_free = c_cw.n

            def _(g, oc=oc, stag_free=stag_free, pv=c_cast.n):
                g.wait_ge(c_cw.sem, stag_free)
                g.wait_ge(c_cast.sem, pv)
                g.dma_start(out=stag[:, 0:INTER], in_=w_down_[oc * 128:(oc + 1) * 128, :]
                            ).then_inc(c_cast.sem, 16)
            em("gpsimd", _)
            cast_done = c_cast.bump(16)

            def _(v, cast_done=cast_done):
                v.wait_ge(c_cast.sem, cast_done)
                for s_ in range(6):
                    srcb = stag if s_ % 2 == 0 else wdb
                    dstb = wdb if s_ % 2 == 0 else stag
                    step = 1 << s_
                    sv = srcb[:, 0:INTER].rearrange(
                        "p (a two b j) -> p a two b j", two=2, b=step, j=128)
                    dv_ = dstb[:, 0:INTER].rearrange(
                        "p (a two b j) -> p a two b j", two=2, b=step, j=128)
                    v.tensor_tensor(dv_[:, :, 0, :, :], sv[:, :, 0, :, :],
                                    sv[:, :, 1, :, :], ALU.add)
                    v.drain()
                    ins = v.tensor_tensor(dv_[:, :, 1, :, :], sv[:, :, 0, :, :],
                                          sv[:, :, 1, :, :], ALU.subtract)
                    v.drain()
                ins.then_inc(c_dve.sem, 1)
            em("vector", _)
            bf_done = c_dve.bump()

            def _(g, oc=oc, bf_done=bf_done, pv=c_cast.n):
                g.wait_ge(c_dve.sem, bf_done)
                g.wait_ge(c_cast.sem, pv)
                g.dma_start(out=down_nat[oc * 128:(oc + 1) * 128, :], in_=stag[:, 0:INTER]
                            ).then_inc(c_cast.sem, 16)
            em("gpsimd", _)
            c_cast.bump(16)
        emit_cache(None, None, c_down_t, rblocks_of(D), INTER, bf_src=down_nat)
        CW_DOWN = c_cw.n
        emit_cache(w_in_, in_nat, c_win_t, rblocks_of(PROJ), D)
        CW_WIN = c_cw.n
        emit_cache(w_out_, out_nat, c_wout_t, rblocks_of(D), I4)
        CW_ALL = c_cw.n

        def _(g):
            g.wait_ge(c_cw.sem, CW_DOWN)
            g.memset(state[:], 0.0).then_inc(c_pool.sem, 1)
        em("gpsimd", _)
        c_pool.bump()

        ring_state = {"head": 0, "acq": []}

        def ring_acquire(nslots, loader, prewait=0):
            s0 = ring_state["head"] % NSLOT
            if s0 + nslots > NSLOT:
                ring_state["acq"].append([NSLOT - s0, ("pad", 0)])
                ring_state["head"] += NSLOT - s0
                s0 = 0
            ent = [nslots, None]
            ring_state["acq"].append(ent)
            ring_state["head"] += nslots
            total = sum(a[0] for a in ring_state["acq"])
            need = total - NSLOT
            waits = {}
            if need > 0:
                acc = 0
                for a in ring_state["acq"][:-1]:
                    acc += a[0]
                    if a[1] is None:
                        raise RuntimeError("ring depth exceeded")
                    sem, val = a[1]
                    if sem != "pad" and val > waits.get(sem, 0):
                        waits[sem] = val
                    if acc >= need:
                        break
            wl = list(waits.items())
            ch = c_wls[s0]

            def _(q, s0=s0, wl=wl, prewait=prewait, ch=ch):
                if prewait > 0:
                    q.wait_ge(c_cw.sem, prewait)
                for sem, val in wl:
                    q.wait_ge(sem, val)
                loader(q, s0, ch)
            em("sync", _)
            return s0, ent, ch

        def slot_ap(s0, off, n):
            return ring[:, s0 * SLOT + off: s0 * SLOT + off + n]

        def emit_ln(src_sbuf, dst_sbuf, su):
            pe_after = {}
            for c in range(DC):
                def _(s, c=c, w=(pe_after.get(c - 2, 0) if c >= 2 else 0)):
                    pass
                sqa = sqscr[:, (c % 2) * T:(c % 2) * T + T]
                w_pe = pe_after[c - 2] if c >= 2 else 0

                def _(s, c=c, sqa=sqa, w_pe=w_pe, src_sbuf=src_sbuf):
                    if w_pe:
                        s.wait_ge(c_pe.sem, w_pe)
                    s.activation(sqa, src_sbuf[:, c * T:(c + 1) * T],
                                 AF.Square, bias=0.0, scale=1.0).then_inc(c_act.sem, 1)
                em("scalar", _)
                sq_done = c_act.bump()

                def _(t, c=c, sqa=sqa, sq_done=sq_done, src_sbuf=src_sbuf):
                    t.wait_ge(c_act.sem, sq_done)
                    t.matmul(pb[4][0:1, 0:T], onesb[:, 0:1],
                             src_sbuf[:, c * T:(c + 1) * T],
                             start=(c == 0), stop=False)
                    t.matmul(pb[4][0:1, T:2 * T], onesf[:, 0:1], sqa,
                             start=False, stop=(c == DC - 1)).then_inc(c_pe.sem, 1)
                em("tensor", _)
                pe_after[c] = c_pe.bump()
            stats_done = c_pe.n

            def _(v, stats_done=stats_done):
                v.wait_ge(c_pe.sem, stats_done)
                mu = strow[0:1, 0:T]
                e2 = strow[0:1, T:2 * T]
                var = strow[0:1, 2 * T:3 * T]
                v.tensor_scalar(mu, pb[4][0:1, 0:T], 1.0 / D, None, ALU.mult)
                v.drain()
                v.tensor_scalar(e2, pb[4][0:1, T:2 * T], 1.0 / D, None, ALU.mult)
                v.drain()
                v.tensor_tensor(var, mu, mu, ALU.mult)
                v.drain()
                v.tensor_tensor(var, e2, var, ALU.subtract).then_inc(c_dve.sem, 1)
                v.drain()
            em("vector", _)
            var_done = c_dve.bump()

            def _(s, var_done=var_done):
                s.wait_ge(c_dve.sem, var_done)
                s.activation(strow[0:1, 3 * T:4 * T], strow[0:1, 2 * T:3 * T],
                             AF.Sqrt, bias=epscol[0:1, 0:1], scale=1.0
                             ).then_inc(c_act.sem, 1)
            em("scalar", _)
            sd_done = c_act.bump()

            def _(v, sd_done=sd_done):
                v.wait_ge(c_act.sem, sd_done)
                a = abrow[0:1, 0:T]
                b = abrow[0:1, T:2 * T]
                v.reciprocal(a, strow[0:1, 3 * T:4 * T])
                v.drain()
                v.scalar_tensor_tensor(b, strow[0:1, 0:T], -1.0, a,
                                       ALU.mult, ALU.mult).then_inc(c_dve.sem, 1)
                v.drain()
            em("vector", _)
            ab_done = c_dve.bump()

            apply_after = {}
            for c in range(DC):
                w_dve = ab_done if c == 0 else apply_after[c - 1]

                def _(t, c=c, w_dve=w_dve, su=su):
                    t.wait_ge(c_dve.sem, w_dve)
                    lhs = surow[0:1, c * 128:(c + 1) * 128] if su else ones1[0:1, :]
                    t.matmul(pb[5][:, 0:2 * T], lhs, abrow[0:1, 0:2 * T],
                             start=True, stop=True).then_inc(c_pe.sem, 1)
                em("tensor", _)
                bc_done = c_pe.bump()

                def _(v, c=c, bc_done=bc_done, src_sbuf=src_sbuf, dst_sbuf=dst_sbuf):
                    v.wait_ge(c_pe.sem, bc_done)
                    v.tensor_tensor(lntmp[:, 0:T], src_sbuf[:, c * T:(c + 1) * T],
                                    pb[5][:, 0:T], ALU.mult)
                    v.drain()
                    v.tensor_tensor(dst_sbuf[:, c * T:(c + 1) * T], lntmp[:, 0:T],
                                    pb[5][:, T:2 * T], ALU.add).then_inc(c_dve.sem, 1)
                em("vector", _)
                apply_after[c] = c_dve.bump()
            barrier()

        def emit_fwht():
            evac_after = {}
            for cp in range(DC):
                w_act = evac_after[cp - 2] if cp >= 2 else 0

                def _(t, cp=cp, w_act=w_act):
                    if w_act:
                        t.wait_ge(c_act.sem, w_act)
                    bank = pb[cp % 2]
                    for c in range(DC):
                        hm = hpos if _h16_sign(cp, c) > 0 else hneg
                        ins = t.matmul(bank[:, 0:T], hm[:, :], usu[:, c * T:(c + 1) * T],
                                       start=(c == 0), stop=(c == DC - 1))
                    ins.then_inc(c_pe.sem, 1)
                em("tensor", _)
                mmdone = c_pe.bump()

                def _(s, cp=cp, mmdone=mmdone):
                    s.wait_ge(c_pe.sem, mmdone)
                    s.activation(vbuf[:, cp * T:(cp + 1) * T], pb[cp % 2][:, 0:T],
                                 AF.Copy, bias=0.0, scale=INV_SQRT_D
                                 ).then_inc(c_act.sem, 1)
                em("scalar", _)
                evac_after[cp] = c_act.bump()
            barrier()

        def emit_phase_i(first_tile):
            h_after, t2_after = {}, {}
            for ic in range(NIC):
                def lu(q, s0, ch, ic=ic):
                    q.dma_start(out=slot_ap(s0, 0, SLOT), in_=c_up_t[ic, :, :]
                                ).then_inc(ch.sem, 16)

                def lg(q, s0, ch, ic=ic):
                    q.dma_start(out=slot_ap(s0, 0, SLOT), in_=c_gate_t[ic, :, :]
                                ).then_inc(ch.sem, 16)
                pw = CW_UG if (first_tile and ic == 0) else 0
                su0, ent_u, ch_u = ring_acquire(1, lu, prewait=pw)
                ld_u = (ch_u.sem, ch_u.bump(16))
                sg0, ent_g, ch_g = ring_acquire(1, lg)
                ld_g = (ch_g.sem, ch_g.bump(16))
                bu, bg, bt = pb[3 * (ic % 2)], pb[1 + 3 * (ic % 2)], pb[2 + 3 * (ic % 2)]
                w_h = h_after[ic - 2] if ic >= 2 else 0
                w_t2 = t2_after[ic - 3] if ic >= 3 else 0

                def _(t, ic=ic, su0=su0, sg0=sg0, ld_u=ld_u, ld_g=ld_g,
                      bu=bu, bg=bg, w_h=w_h, w_t2=w_t2):
                    if w_h:
                        t.wait_ge(c_dve.sem, w_h)
                        t.wait_ge(c_act.sem, w_t2)
                    t.wait_ge(ld_u[0], ld_u[1])
                    for dc in range(DC):
                        ins = t.matmul(bu[:, 0:T], slot_ap(su0, dc * 128, 128),
                                       vbuf[:, dc * T:(dc + 1) * T],
                                       start=(dc == 0), stop=(dc == DC - 1))
                    ins.then_inc(c_wfree.sem, 1)
                    t.wait_ge(ld_g[0], ld_g[1])
                    for dc in range(DC):
                        ins = t.matmul(bg[:, 0:T], slot_ap(sg0, dc * 128, 128),
                                       vbuf[:, dc * T:(dc + 1) * T],
                                       start=(dc == 0), stop=(dc == DC - 1))
                    ins.then_inc(c_wfree.sem, 1)
                em("tensor", _)
                ent_u[1] = (c_wfree.sem, c_wfree.bump())
                ent_g[1] = (c_wfree.sem, c_wfree.bump())
                upgate_done = ent_g[1][1]

                def _(s, ic=ic, bg=bg, upgate_done=upgate_done):
                    s.wait_ge(c_wfree.sem, upgate_done)
                    s.activation(sgscr[:, (ic % 2) * T:(ic % 2) * T + T], bg[:, 0:T],
                                 AF.Silu, bias=0.0, scale=s_gate[:, ic:ic + 1]
                                 ).then_inc(c_act.sem, 1)
                em("scalar", _)
                sg_done = c_act.bump()

                def _(v, ic=ic, bu=bu, sg_done=sg_done):
                    v.wait_ge(c_act.sem, sg_done)
                    v.scalar_tensor_tensor(
                        usu[:, (ic % DC) * T:(ic % DC) * T + T], bu[:, 0:T],
                        s_upsu[:, ic:ic + 1],
                        sgscr[:, (ic % 2) * T:(ic % 2) * T + T],
                        ALU.mult, ALU.mult).then_inc(c_dve.sem, 1)
                em("vector", _)
                h_after[ic] = c_dve.bump()

                def emit_t2(jc):
                    bt2 = pb[2 + 3 * (jc % 2)]
                    h_done = h_after[jc]

                    w_bt = t2_after.get(jc - 2, 0)

                    def _(t, jc=jc, bt2=bt2, h_done=h_done, w_bt=w_bt):
                        t.wait_ge(c_dve.sem, h_done)
                        if w_bt:
                            t.wait_ge(c_act.sem, w_bt)
                        t.matmul(bt2[:, 0:T], hpos[:, :],
                                 usu[:, (jc % DC) * T:(jc % DC) * T + T],
                                 start=True, stop=True).then_inc(c_pe.sem, 1)
                    em("tensor", _)
                    t2mm_done = c_pe.bump()

                    def _(s, jc=jc, bt2=bt2, t2mm_done=t2mm_done):
                        s.wait_ge(c_pe.sem, t2mm_done)
                        s.activation(t2g[:, jc * T:(jc + 1) * T], bt2[:, 0:T],
                                     AF.Copy, bias=0.0, scale=1.0).then_inc(c_act.sem, 1)
                    em("scalar", _)
                    t2_after[jc] = c_act.bump()

                if ic >= 1:
                    emit_t2(ic - 1)
                if ic == NIC - 1:
                    emit_t2(ic)
            barrier()

        def emit_phase_ii(first_tile):
            res_after = {}
            for o in range(DC):
                def ld(q, s0, ch, o=o):
                    q.dma_start(out=slot_ap(s0, 0, 4 * SLOT), in_=c_down_t[o, :, :]
                                ).then_inc(ch.sem, 16)
                pw = CW_DOWN if (first_tile and o == 0) else 0
                s0, ent, ch_ = ring_acquire(4, ld, prewait=pw)
                ldd = (ch_.sem, ch_.bump(16))
                bank = pb[o % 2]
                w_res = res_after[o - 2] if o >= 2 else 0

                def _(t, o=o, s0=s0, ldd=ldd, bank=bank, w_res=w_res):
                    if w_res:
                        t.wait_ge(c_dve.sem, w_res)
                    t.wait_ge(ldd[0], ldd[1])
                    for ic in range(NIC):
                        ins = t.matmul(bank[:, 0:T], slot_ap(s0, ic * 128, 128),
                                       t2g[:, ic * T:(ic + 1) * T],
                                       start=(ic == 0), stop=(ic == NIC - 1))
                    ins.then_inc(c_wfree.sem, 4)
                em("tensor", _)
                ent[1] = (c_wfree.sem, c_wfree.bump(4))
                mmdone = ent[1][1]

                def _(v, o=o, bank=bank, mmdone=mmdone):
                    v.wait_ge(c_wfree.sem, mmdone)
                    v.scalar_tensor_tensor(
                        xres[:, o * T:(o + 1) * T], bank[:, 0:T], sdp2[:, o:o + 1],
                        xres[:, o * T:(o + 1) * T], ALU.mult, ALU.add
                    ).then_inc(c_dve.sem, 1)
                em("vector", _)
                res_after[o] = c_dve.bump()
            barrier()

        def emit_proj(first_tile):
            if first_tile:
                for e in COMPUTE:
                    def _(q, cwa=CW_ALL):
                        q.wait_ge(c_cw.sem, cwa)
                    em(e, _)
            order = [66] + list(range(66))
            evac_after = {}
            conv_mm_after = {}
            conv_act_after = {}
            cdslot = {}

            def scr(cc):
                return xbcscr[:, (cc % 2) * (T + 3):(cc % 2) * (T + 3) + T + 3]

            for idx, jb in enumerate(order):
                def ld(q, s0, ch, jb=jb):
                    if jb == 66:
                        q.dma_start(
                            out=slot_ap(s0, 0, SLOT).rearrange(
                                "p (c r) -> p c r", c=DC)[:, :, 0:64],
                            in_=c_win_t[jb, :, :].rearrange(
                                "p (c r) -> p c r", c=DC)[:, :, 0:64]
                        ).then_inc(ch.sem, 16)
                    else:
                        q.dma_start(out=slot_ap(s0, 0, SLOT), in_=c_win_t[jb, :, :]
                                    ).then_inc(ch.sem, 16)
                pw = CW_WIN if (first_tile and idx == 0) else 0
                s0, ent, ch_ = ring_acquire(1, ld, prewait=pw)
                ldd = (ch_.sem, ch_.bump(16))
                bank = pb[idx % 2]
                w_ev = evac_after[idx - 2] if idx >= 2 else None

                def _(t, jb=jb, s0=s0, ldd=ldd, bank=bank, w_ev=w_ev):
                    if w_ev is not None:
                        t.wait_ge(w_ev[0], w_ev[1])
                    t.wait_ge(ldd[0], ldd[1])
                    M = 64 if jb == 66 else 128
                    for dc in range(DC):
                        ins = t.matmul(bank[0:M, 0:T], slot_ap(s0, dc * 128, M),
                                       usu[:, dc * T:(dc + 1) * T],
                                       start=(dc == 0), stop=(dc == DC - 1))
                    ins.then_inc(c_wfree.sem, 1)
                em("tensor", _)
                ent[1] = (c_wfree.sem, c_wfree.bump())
                mmdone = ("wfree", ent[1][1])

                if jb == 66:
                    def _(s, bank=bank, mmdone=mmdone):
                        s.wait_ge(c_wfree.sem, mmdone[1])
                        s.activation(dtraw[:, :], bank[0:64, 0:T], AF.Copy,
                                     bias=0.0, scale=1.0).then_inc(c_act.sem, 1)
                    em("scalar", _)
                    evac_after[idx] = (c_act.sem, c_act.bump())
                elif jb < 32:
                    def _(s, jb=jb, bank=bank, mmdone=mmdone):
                        s.wait_ge(c_wfree.sem, mmdone[1])
                        s.activation(sg_sl(jb), bank[:, 0:T], AF.Silu,
                                     bias=0.0, scale=1.0).then_inc(c_act.sem, 1)
                    em("scalar", _)
                    evac_after[idx] = (c_act.sem, c_act.bump())
                else:
                    cc = jb - 32
                    if cc % 4 == 0:
                        def ldc(q, s0, ch, cc=cc, cdd=CD_DONE):
                            q.wait_ge(c_cds.sem, cdd)
                            n_ = min(4, NCC - cc)
                            q.dma_start(out=slot_ap(s0, 0, n_ * 512).rearrange(
                                            "p (c r) -> p c r", c=n_),
                                        in_=cdiag[cc:cc + n_, :, :].transpose([1, 0, 2])
                                        ).then_inc(ch.sem, 16)
                        cds0, cdent, cdch = ring_acquire(1, ldc)
                        cdld = (cdch.sem, cdch.bump(16))
                        cdslot.clear()
                        cdslot.update({"s0": cds0, "ent": cdent, "ld": cdld, "base": cc})
                    w_cm = conv_mm_after[cc - 2] if cc >= 2 else 0

                    def _(s, cc=cc, bank=bank, mmdone=mmdone, w_cm=w_cm):
                        s.wait_ge(c_wfree.sem, mmdone[1])
                        if w_cm:
                            s.wait_ge(c_pe.sem, w_cm)
                        s.activation(scr(cc)[:, 3:T + 3], bank[:, 0:T],
                                     AF.Copy, bias=0.0, scale=1.0).then_inc(c_act.sem, 1)
                    em("scalar", _)
                    raw_done = c_act.bump()
                    evac_after[idx] = (c_act.sem, raw_done)

                    def _(v, cc=cc, raw_done=raw_done):
                        v.wait_ge(c_act.sem, raw_done)
                        v.tensor_copy(scr(cc)[:, 0:3], cbuf[:, cc * 3:cc * 3 + 3])
                        v.drain()
                        v.tensor_copy(cbuf[:, cc * 3:cc * 3 + 3], scr(cc)[:, T:T + 3]
                                      ).then_inc(c_dve.sem, 1)
                        v.drain()
                    em("vector", _)
                    carry_done = c_dve.bump()
                    w_ca = conv_act_after[cc - 2] if cc >= 2 else 0
                    cds0_, cdld_, cdbase_ = cdslot["s0"], cdslot["ld"], cdslot["base"]

                    def _(t, cc=cc, carry_done=carry_done, w_ca=w_ca,
                          cds0_=cds0_, cdld_=cdld_, cdbase_=cdbase_):
                        t.wait_ge(c_dve.sem, carry_done)
                        t.wait_ge(cdld_[0], cdld_[1])
                        if w_ca:
                            t.wait_ge(c_act.sem, w_ca)
                        off = (cc - cdbase_) * 512
                        for k in range(4):
                            ins = t.matmul(pb[2 + cc % 2][:, 0:T],
                                           slot_ap(cds0_, off + k * 128, 128),
                                           scr(cc)[:, k:k + T],
                                           start=(k == 0), stop=(k == 3))
                        ins.then_inc(c_pe.sem, 1)
                    em("tensor", _)
                    conv_mm_after[cc] = c_pe.bump()
                    if cc % 4 == 3 or cc == NCC - 1:
                        cdslot["ent"][1] = (c_pe.sem, conv_mm_after[cc])
                    cv_done = conv_mm_after[cc]

                    def _(s, cc=cc, cv_done=cv_done):
                        s.wait_ge(c_pe.sem, cv_done)
                        s.activation(xact[:, cc * T:(cc + 1) * T],
                                     pb[2 + cc % 2][:, 0:T], AF.Silu,
                                     bias=cbfm[:, cc:cc + 1], scale=1.0
                                     ).then_inc(c_act.sem, 1)
                    em("scalar", _)
                    conv_act_after[cc] = c_act.bump()
            barrier()

        def emit_dtpipe():
            def _(s):
                s.activation(ldaf[:, :], dtraw[:, :], AF.Relu, bias=dtbcol[:, :], scale=1.0)
                s.activation(cumf[:, :], dtraw[:, :], AF.Abs, bias=dtbcol[:, :], scale=1.0)
                s.drain()
                s.activation(cumf[:, :], cumf[:, :], AF.Exp, bias=0.0, scale=-1.0)
                s.drain()
                s.activation(cumf[:, :], cumf[:, :], AF.Ln,
                             bias=onecol64[:, :], scale=1.0).then_inc(c_act.sem, 1)
            em("scalar", _)
            sp_done = c_act.bump()

            def _(v, sp_done=sp_done):
                v.wait_ge(c_act.sem, sp_done)
                v.tensor_tensor(dtp[:, :], ldaf[:, :], cumf[:, :], ALU.add)
                v.drain()
                v.tensor_scalar_min(dtp[:, :], dtp[:, :], 1.0e6)
                v.drain()
                v.tensor_scalar(ldaf[:, :], dtp[:, :], acol[:, 0:1], None, ALU.mult)
                v.drain()
                for l_ in range(NCH):
                    v.tensor_tensor_scan(cumf[:, l_ * L:(l_ + 1) * L],
                                         ldaf[:, l_ * L:(l_ + 1) * L],
                                         zero64[:, :], 0.0, ALU.add, ALU.add)
                    v.drain()
                v.tensor_copy(wff[:, 0:1], cumf[:, 0:1]).then_inc(c_dve.sem, 1)
            em("vector", _)
            cum_done = c_dve.bump()

            def _(s, cum_done=cum_done):
                s.wait_ge(c_dve.sem, cum_done)
                for l_ in range(NCH):
                    ins = s.activation(decf[:, l_:l_ + 1],
                                       cumf[:, l_ * L + L - 1:l_ * L + L],
                                       AF.Exp, bias=0.0, scale=1.0)
                ins.then_inc(c_act.sem, 1)
            em("scalar", _)
            dec_done = c_act.bump()

            def _(g, cum_done=cum_done, dec_done=dec_done):
                g.wait_ge(c_dve.sem, cum_done)
                g.dma_start(out=cumscr[:, :], in_=cumf[:, :]).then_inc(c_bnc.sem, 16)
                g.wait_ge(c_act.sem, dec_done)
                for l_ in range(NCH):
                    g.dma_start(out=decscr[l_, :].unsqueeze(1), in_=decf[:, l_:l_ + 1]
                                ).then_inc(c_bnc.sem, 16)
            em("gpsimd", _)
            c_bnc.bump(16 * (1 + NCH))
            barrier(extra=(c_bnc,))

        def emit_ssd_chunk(l_):
            t0 = l_ * L

            def _(g, l_=l_):
                g.dma_start(out=decrow[0:1, :], in_=decscr[l_, :].unsqueeze(0)
                            ).then_inc(c_dec.sem, 16)
            em("gpsimd", _)
            dec_ld = c_dec.bump(16)

            def _(t, t0=t0):
                t.matmul(pb[4][:, 0:128], xact[:, NXC * T + t0:NXC * T + t0 + L],
                         xact[:, (NXC + 1) * T + t0:(NXC + 1) * T + t0 + L],
                         start=True, stop=True)
                t.transpose(pbt[:, 0:128], xact[:, NXC * T + t0:NXC * T + t0 + L],
                            eyeb[:, :]).then_inc(c_pe.sem, 1)
            em("tensor", _)
            gt_done = c_pe.bump()

            def _(v, gt_done=gt_done):
                v.wait_ge(c_pe.sem, gt_done)
                v.tensor_tensor(gtm[:, :], pb[4][:, 0:128], masku[:, :], ALU.mult)
                v.drain()
                v.tensor_copy(btm[:, :], pbt[:, 0:128]).then_inc(c_dve.sem, 1)
            em("vector", _)
            gtm_done = c_dve.bump()

            def _(v, l_=l_):
                v.tensor_scalar(wff[:, :], cumf[:, l_ * L:(l_ + 1) * L], -1.0,
                                None, ALU.mult)
                v.drain()
                v.tensor_scalar(wff[:, :], wff[:, :],
                                cumf[:, l_ * L + L - 1:l_ * L + L], None, ALU.add
                                ).then_inc(c_dve.sem, 1)
                v.drain()
            em("vector", _)
            wf_done = c_dve.bump()

            def _(s, wf_done=wf_done):
                s.wait_ge(c_dve.sem, wf_done)
                s.drain()
                s.activation(wff[:, :], wff[:, :], AF.Exp, bias=0.0, scale=1.0
                             ).then_inc(c_act.sem, 1)
            em("scalar", _)
            wexp_done = c_act.bump()

            def _(t, l_=l_, wexp_done=wexp_done, gtm_done=gtm_done):
                t.wait_ge(c_act.sem, wexp_done)
                t.wait_ge(c_dve.sem, gtm_done)
                t.transpose(pb[6][:, 0:64], wff[:, :], eyef[0:64, 0:64])
                t.transpose(pb[6][:, 64:128], dtp[:, l_ * L:(l_ + 1) * L],
                            eyef[0:64, 0:64]).then_inc(c_pe.sem, 1)
            em("tensor", _)
            wtr_done = c_pe.bump()

            def _(v, wtr_done=wtr_done):
                v.wait_ge(c_pe.sem, wtr_done)
                v.tensor_copy(wtm[:, :], pb[6][:, 0:64])
                v.drain()
                v.tensor_copy(dtptm[:, :], pb[6][:, 64:128]).then_inc(c_dve.sem, 1)
                v.drain()
            em("vector", _)
            wtm_done = c_dve.bump()

            xtr_after = {}
            for grp in range(4):
                w_prev = gtm_done if grp == 0 else xtr_after[grp - 1]

                def _(t, grp=grp, t0=t0, w_prev=w_prev):
                    t.wait_ge(c_dve.sem, w_prev)
                    for j in range(8):
                        c = grp * 8 + j
                        ins = t.transpose(pbt[:, j * 128:(j + 1) * 128],
                                          xact[:, c * T + t0:c * T + t0 + L], eyeb[:, :])
                    ins.then_inc(c_pe.sem, 1)
                em("tensor", _)
                tr_done = c_pe.bump()

                def _(v, grp=grp, tr_done=tr_done):
                    v.wait_ge(c_pe.sem, tr_done)
                    v.tensor_copy(xtm[:, grp * 1024:(grp + 1) * 1024], pbt[:, :]
                                  ).then_inc(c_dve.sem, 1)
                em("vector", _)
                xtr_after[grp] = c_dve.bump()
            xtr_done = c_dve.n

            def _(v, xtr_done=xtr_done, wtm_done=wtm_done):
                v.wait_ge(c_dve.sem, xtr_done)
                v.wait_ge(c_dve.sem, wtm_done)
                drep = dtptm[:, :].unsqueeze(2).broadcast_to([128, 64, 64])
                v.tensor_tensor(xtm[:, :].rearrange("p (h q) -> p h q", h=64),
                                xtm[:, :].rearrange("p (h q) -> p h q", h=64),
                                drep, ALU.mult)
                v.drain()
                wrep = wtm[:, :].unsqueeze(2).broadcast_to([128, 64, 64])
                v.tensor_tensor(
                    vbuf[:, 0:I4].rearrange("p (h q) -> p h q", h=64),
                    xtm[:, :].rearrange("p (h q) -> p h q", h=64), wrep, ALU.mult
                ).then_inc(c_dve.sem, 1)
                v.drain()
            em("vector", _)
            xw_done = c_dve.bump()

            exp_after = {}
            min_after = {}
            grp_pe_after = {}
            yevac_after = {}
            for g4 in range(16):
                w_pe = grp_pe_after[g4 - 2] if g4 >= 2 else 0

                def _(gq, g4=g4, l_=l_, w_pe=w_pe):
                    if w_pe:
                        gq.wait_ge(c_pe.sem, w_pe)
                    src = cumscr[4 * g4:4 * g4 + 4, l_ * L:(l_ + 1) * L]
                    gq.dma_start(out=cumrow[0:1, (g4 % 2) * 512:(g4 % 2) * 512 + 512]
                                 .rearrange("o (h t) -> o h t", h=4),
                                 in_=src.unsqueeze(0)).then_inc(c_row2[g4 % 2].sem, 16)
                em("gpsimd", _)
                row_ld = c_row2[g4 % 2].bump(16)
                w_min = min_after[g4 - 1] if g4 >= 1 else 0
                w_exp = exp_after[g4 - 1] if g4 >= 1 else 0

                def _(t, g4=g4, row_ld=row_ld, xw_done=xw_done, w_exp=w_exp, w_min=w_min):
                    t.wait_ge(c_row2[g4 % 2].sem, row_ld)
                    if g4 == 0:
                        t.wait_ge(c_dve.sem, xw_done)
                    if w_min:
                        t.wait_ge(c_dve.sem, w_min)
                    if w_exp:
                        t.wait_ge(c_act.sem, w_exp)
                    cr = cumrow[0:1, (g4 % 2) * 512:(g4 % 2) * 512 + 512]
                    t.matmul(pb[3][:, 0:512], ones1[0:1, :], cr, start=True, stop=False)
                    for k in range(4):
                        ins = t.matmul(pb[3][:, k * 128:(k + 1) * 128],
                                       cr[0:1, k * 128:(k + 1) * 128], nones1[0:1, :],
                                       start=False, stop=(k == 3))
                    ins.then_inc(c_pe.sem, 1)
                    t.matmul(pb[4][:, 0:512], ones1[0:1, :], cr,
                             start=True, stop=True).then_inc(c_pe.sem, 1)
                em("tensor", _)
                c_pe.bump()
                dmat_done = c_pe.bump()
                w_eslot = exp_after[g4 - 2] if g4 >= 2 else 0

                def _(v, g4=g4, dmat_done=dmat_done, w_eslot=w_eslot):
                    v.wait_ge(c_pe.sem, dmat_done)
                    if w_eslot:
                        v.wait_ge(c_act.sem, w_eslot)
                    v.tensor_scalar_min(mexpf[:, (g4 % 2) * 512:(g4 % 2) * 512 + 512],
                                        pb[3][:, 0:512], 0.0).then_inc(c_dve.sem, 1)
                    v.drain()
                em("vector", _)
                min_after[g4] = c_dve.bump()
                mind = min_after[g4]
                w_slot = grp_pe_after[g4 - 2] if g4 >= 2 else 0

                def _(s, g4=g4, mind=mind, w_slot=w_slot):
                    s.wait_ge(c_dve.sem, mind)
                    if w_slot:
                        s.wait_ge(c_pe.sem, w_slot)
                    s.activation(mgt[:, (g4 % 2) * 512:(g4 % 2) * 512 + 512],
                                 mexpf[:, (g4 % 2) * 512:(g4 % 2) * 512 + 512],
                                 AF.Exp, bias=0.0, scale=1.0)
                    s.activation(ebs[:, (g4 % 2) * 512:(g4 % 2) * 512 + 512],
                                 pb[4][:, 0:512], AF.Exp, bias=0.0, scale=1.0
                                 ).then_inc(c_act.sem, 1)
                em("scalar", _)
                exp_after[g4] = c_act.bump()
                expd = exp_after[g4]

                def _(v, g4=g4, l_=l_, expd=expd):
                    v.wait_ge(c_act.sem, expd)
                    sl_m = mgt[:, (g4 % 2) * 512:(g4 % 2) * 512 + 512]
                    sl_c = cexp[:, (g4 % 2) * 512:(g4 % 2) * 512 + 512]
                    sl_e = ebs[:, (g4 % 2) * 512:(g4 % 2) * 512 + 512]
                    grep = gtm[:, :].unsqueeze(1).broadcast_to([128, 4, 128])
                    v.tensor_tensor(sl_m.rearrange("p (h t) -> p h t", h=4),
                                    sl_m.rearrange("p (h t) -> p h t", h=4),
                                    grep, ALU.mult)
                    v.drain()
                    crep = xact[:, (NXC + 1) * T + l_ * L:(NXC + 1) * T + l_ * L + L] \
                        .unsqueeze(1).broadcast_to([128, 4, 128])
                    v.tensor_tensor(sl_c.rearrange("p (h t) -> p h t", h=4), crep,
                                    sl_e.rearrange("p (h t) -> p h t", h=4), ALU.mult
                                    ).then_inc(c_dve.sem, 1)
                    v.drain()
                em("vector", _)
                combd = c_dve.bump()

                ybank = pb[5 + (g4 % 2)]
                w_ye = yevac_after[g4 - 2] if g4 >= 2 else 0

                def _(t, g4=g4, t0=t0, ybank=ybank, combd=combd, w_ye=w_ye):
                    t.wait_ge(c_dve.sem, combd)
                    if w_ye:
                        t.wait_ge(c_dve.sem, w_ye)
                    sl_m = mgt[:, (g4 % 2) * 512:(g4 % 2) * 512 + 512]
                    sl_c = cexp[:, (g4 % 2) * 512:(g4 % 2) * 512 + 512]
                    mms = [(ybank[:, 0:128], dddiag[:, 2 * g4 * 128:(2 * g4 + 1) * 128],
                            xact[:, 2 * g4 * T + t0:2 * g4 * T + t0 + L])]
                    for k in range(4):
                        h = 4 * g4 + k
                        po = 64 * (h % 2)
                        pj = (k // 2) * 128
                        mms.append((ybank[po:po + 64, pj:pj + 128],
                                    xtm[:, h * 64:h * 64 + 64],
                                    sl_m[:, k * 128:(k + 1) * 128]))
                        mms.append((ybank[po:po + 64, pj:pj + 128],
                                    state16[:, h * 64:h * 64 + 64],
                                    sl_c[:, k * 128:(k + 1) * 128]))
                    c2 = 2 * g4 + 1
                    mms.append((ybank[:, 128:256], dddiag[:, c2 * 128:(c2 + 1) * 128],
                                xact[:, c2 * T + t0:c2 * T + t0 + L]))
                    for i, (o_, l2, r2) in enumerate(mms):
                        ins = t.matmul(o_, l2, r2, start=(i == 0), stop=(i == len(mms) - 1))
                    ins.then_inc(c_pe.sem, 1)
                em("tensor", _)
                grp_pe_after[g4] = c_pe.bump()
                ymm_done = grp_pe_after[g4]

                def _(v, g4=g4, t0=t0, ybank=ybank, ymm_done=ymm_done):
                    v.wait_ge(c_pe.sem, ymm_done)
                    dst = xact[:, :].rearrange("p (c t) -> p c t", c=NXC + 2)[
                        :, 2 * g4:2 * g4 + 2, t0:t0 + L]
                    sg2 = t2g[:, :].bitcast(F32).rearrange("p (c t) -> p c t", c=NXC)[
                        :, 2 * g4:2 * g4 + 2, t0:t0 + L]
                    v.tensor_tensor(dst, ybank[:, 0:256].rearrange("p (c t) -> p c t", c=2),
                                    sg2, ALU.mult).then_inc(c_dve.sem, 1)
                em("vector", _)
                yevac_after[g4] = c_dve.bump()

            scomb_after = {}
            exp_last = exp_after[15]
            for g8 in range(8):
                w_sc = scomb_after[g8 - 1] if g8 >= 1 else 0

                def _(t, g8=g8, dec_ld=dec_ld, w_sc=w_sc, exp_last=exp_last):
                    t.wait_ge(c_dec.sem, dec_ld)
                    if g8 == 0:
                        t.wait_ge(c_act.sem, exp_last)
                    if w_sc:
                        t.wait_ge(c_dve.sem, w_sc)
                    t.matmul(pb[3][:, 0:512], btm[:, :],
                             vbuf[:, g8 * 512:(g8 + 1) * 512],
                             start=True, stop=True).then_inc(c_pe.sem, 1)
                    dsl = decrow[0:1, g8 * 8:(g8 + 1) * 8].unsqueeze(2) \
                        .broadcast_to([1, 8, 64])
                    t.matmul(pb[4][:, 0:512].rearrange("p (h q) -> p h q", h=8),
                             ones1[0:1, :], dsl, start=True, stop=True
                             ).then_inc(c_pe.sem, 1)
                em("tensor", _)
                c_pe.bump()
                smm_done = c_pe.bump()

                def _(v, g8=g8, smm_done=smm_done):
                    v.wait_ge(c_pe.sem, smm_done)
                    v.tensor_tensor(stmp[:, :], state[:, g8 * 512:(g8 + 1) * 512],
                                    pb[4][:, 0:512], ALU.mult)
                    v.drain()
                    v.tensor_tensor(state[:, g8 * 512:(g8 + 1) * 512],
                                    stmp[:, :], pb[3][:, 0:512], ALU.add)
                    v.drain()
                    v.tensor_copy(state16[:, g8 * 512:(g8 + 1) * 512],
                                  state[:, g8 * 512:(g8 + 1) * 512]
                                  ).then_inc(c_dve.sem, 1)
                    v.drain()
                em("vector", _)
                scomb_after[g8] = c_dve.bump()
            barrier()

        def emit_mamba_tail(first_tile):
            pe_after = {}
            for c in range(NXC):
                w_pe = pe_after[c - 2] if c >= 2 else 0

                def _(s, c=c, w_pe=w_pe):
                    if w_pe:
                        s.wait_ge(c_pe.sem, w_pe)
                    s.activation(sqscr[:, (c % 2) * T:(c % 2) * T + T],
                                 xact[:, c * T:(c + 1) * T], AF.Square,
                                 bias=0.0, scale=1.0).then_inc(c_act.sem, 1)
                em("scalar", _)
                sq_done = c_act.bump()

                def _(t, c=c, sq_done=sq_done):
                    t.wait_ge(c_act.sem, sq_done)
                    t.matmul(pb[4][0:1, 0:T], onesf[:, 0:1],
                             sqscr[:, (c % 2) * T:(c % 2) * T + T],
                             start=(c == 0), stop=(c == NXC - 1)).then_inc(c_pe.sem, 1)
                em("tensor", _)
                pe_after[c] = c_pe.bump()
            stats_done = c_pe.n

            def _(v, stats_done=stats_done):
                v.wait_ge(c_pe.sem, stats_done)
                v.tensor_scalar(strow[0:1, 0:T], pb[4][0:1, 0:T], 1.0 / I4,
                                None, ALU.mult).then_inc(c_dve.sem, 1)
                v.drain()
            em("vector", _)
            var_done = c_dve.bump()

            def _(s, var_done=var_done):
                s.wait_ge(c_dve.sem, var_done)
                s.activation(strow[0:1, T:2 * T], strow[0:1, 0:T], AF.Sqrt,
                             bias=epscol[0:1, 1:2], scale=1.0).then_inc(c_act.sem, 1)
            em("scalar", _)
            sd_done = c_act.bump()

            def _(v, sd_done=sd_done):
                v.wait_ge(c_act.sem, sd_done)
                v.reciprocal(strow[0:1, 2 * T:3 * T], strow[0:1, T:2 * T]
                             ).then_inc(c_dve.sem, 1)
                v.drain()
            em("vector", _)
            rstd_done = c_dve.bump()

            def _(t, rstd_done=rstd_done):
                t.wait_ge(c_dve.sem, rstd_done)
                t.matmul(pb[4][:, 0:T], ones1[0:1, :], strow[0:1, 2 * T:3 * T],
                         start=True, stop=True).then_inc(c_pe.sem, 1)
            em("tensor", _)
            rst_done = c_pe.bump()

            for c in range(NXC):
                w0 = rst_done if c == 0 else 0

                def _(v, c=c, w0=w0):
                    if w0:
                        v.wait_ge(c_pe.sem, w0)
                    v.scalar_tensor_tensor(
                        xact[:, c * T:(c + 1) * T], xact[:, c * T:(c + 1) * T],
                        nwfm[:, c:c + 1], pb[4][:, 0:T], ALU.mult, ALU.mult
                    ).then_inc(c_dve.sem, 1)
                em("vector", _)
                c_dve.bump()
            yn_done = c_dve.n

            res_after = {}
            for o in range(DC):
                def ld(q, s0, ch, o=o):
                    q.dma_start(out=slot_ap(s0, 0, 2 * SLOT), in_=c_wout_t[o, :, :]
                                ).then_inc(ch.sem, 16)
                pw = CW_ALL if (first_tile and o == 0) else 0
                s0, ent, ch_ = ring_acquire(2, ld, prewait=pw)
                ldd = (ch_.sem, ch_.bump(16))
                bank = pb[o % 2]
                w0 = yn_done if o == 0 else 0
                w_res = res_after[o - 2] if o >= 2 else 0

                def _(t, o=o, s0=s0, ldd=ldd, bank=bank, w0=w0, w_res=w_res):
                    if w0:
                        t.wait_ge(c_dve.sem, w0)
                    if w_res:
                        t.wait_ge(c_dve.sem, w_res)
                    t.wait_ge(ldd[0], ldd[1])
                    for c in range(NXC):
                        ins = t.matmul(bank[:, 0:T], slot_ap(s0, c * 128, 128),
                                       xact[:, c * T:(c + 1) * T],
                                       start=(c == 0), stop=(c == NXC - 1))
                    ins.then_inc(c_wfree.sem, 2)
                em("tensor", _)
                ent[1] = (c_wfree.sem, c_wfree.bump(2))
                mmdone = ent[1][1]

                def _(v, o=o, bank=bank, mmdone=mmdone):
                    v.wait_ge(c_wfree.sem, mmdone)
                    v.scalar_tensor_tensor(
                        xres[:, o * T:(o + 1) * T], bank[:, 0:T], 1.0,
                        xres[:, o * T:(o + 1) * T], ALU.mult, ALU.add
                    ).then_inc(c_dve.sem, 1)
                em("vector", _)
                res_after[o] = c_dve.bump()
            barrier()

        def emit_store(tile):
            def _(g, tile=tile):
                g.dma_start(out=out.rearrange("(c p) s -> p c s", p=128)[
                    :, :, tile * T:(tile + 1) * T],
                    in_=xres[:, :].rearrange("p (c t) -> p c t", c=DC)
                ).then_inc(c_st.sem, 16)
            em("gpsimd", _)
            c_st.bump(16)

        NT_EFF = 1 if stage >= 40 else NT
        for tile in range(NT_EFF):
            def _(g, tile=tile):
                if tile > 0:
                    g.wait_ge(c_st.sem, tile * 16)
                g.dma_start(out=xres[:, :].rearrange("p (c t) -> p c t", c=DC),
                            in_=x_in.rearrange("(c p) s -> p c s", p=128)[
                                :, :, tile * T:(tile + 1) * T]).then_inc(c_x.sem, 16)
            em("gpsimd", _)
            x_ld = c_x.bump(16)
            for e in COMPUTE:
                def _(q, x_ld=x_ld):
                    q.wait_ge(c_x.sem, x_ld)
                em(e, _)

            emit_ln(xres, usu, su=True)
            emit_fwht()
            emit_phase_i(first_tile=(tile == 0))
            emit_phase_ii(first_tile=(tile == 0))
            if stage == 1:
                emit_store(tile)
                continue

            emit_ln(xres, usu, su=False)
            emit_proj(first_tile=(tile == 0))
            emit_dtpipe()
            if stage == 45:
                def _(g, tile=tile):
                    g.dma_start(out=out.rearrange("(c p) s -> p c s", p=128)[
                        :, :, tile * T:(tile + 1) * T],
                        in_=usu[:, :].rearrange("p (c t) -> p c t", c=DC)
                    ).then_inc(c_st.sem, 16)
                em("gpsimd", _)
                c_st.bump(16)
                def _(g):
                    g.wait_ge(c_cw.sem, CW_ALL)
                    g.wait_ge(c_st.sem, c_st.n)
                em("gpsimd", _)
                break
            if stage in (46, 47):
                def _(g, tile=tile, k=stage - 46):
                    g.dma_start(out=out.rearrange("(c p) s -> p c s", p=128)[
                        :, :, tile * T:(tile + 1) * T],
                        in_=t2g[:, :].bitcast(F32).rearrange(
                            "p (c t) -> p c t", c=NXC)[:, k * 16:(k + 1) * 16, :]
                    ).then_inc(c_st.sem, 16)
                em("gpsimd", _)
                c_st.bump(16)
                def _(g):
                    g.wait_ge(c_cw.sem, CW_ALL)
                    g.wait_ge(c_st.sem, c_st.n)
                em("gpsimd", _)
                break
            if stage in (40, 41):
                def _(g, tile=tile, k=stage - 40):
                    g.dma_start(out=out.rearrange("(c p) s -> p c s", p=128)[
                        :, :, tile * T:(tile + 1) * T],
                        in_=xact[:, :].rearrange("p (c t) -> p c t", c=NXC + 2)[
                            :, k * 16:(k + 1) * 16, :]
                    ).then_inc(c_st.sem, 16)
                em("gpsimd", _)
                c_st.bump(16)
                def _(g):
                    g.wait_ge(c_cw.sem, CW_ALL)
                    g.wait_ge(c_st.sem, c_st.n)
                em("gpsimd", _)
                break
            for l_ in range(NCH):
                emit_ssd_chunk(l_)
            if stage in (42, 43):
                def _(g, tile=tile, k=stage - 42):
                    g.dma_start(out=out.rearrange("(c p) s -> p c s", p=128)[
                        :, :, tile * T:(tile + 1) * T],
                        in_=xact[:, :].rearrange("p (c t) -> p c t", c=NXC + 2)[
                            :, k * 16:(k + 1) * 16, :]
                    ).then_inc(c_st.sem, 16)
                em("gpsimd", _)
                c_st.bump(16)
                def _(g):
                    g.wait_ge(c_cw.sem, CW_ALL)
                    g.wait_ge(c_st.sem, c_st.n)
                em("gpsimd", _)
                break
            emit_mamba_tail(first_tile=(tile == 0))
            if stage == 2:
                emit_store(tile)
                continue

            emit_ln(xres, usu, su=True)
            emit_fwht()
            emit_phase_i(first_tile=False)
            emit_phase_ii(first_tile=False)
            emit_store(tile)

        def _(g, cwa=None):
            g.wait_ge(c_cw.sem, CW_ALL)
            g.wait_ge(c_st.sem, c_st.n)
        em("gpsimd", _)

        @block.sync
        def _(sync):
            for fn in prog.q["sync"]:
                fn(sync)

        @block.gpsimd
        def _(g):
            for fn in prog.q["gpsimd"]:
                fn(g)

        @block.vector
        def _(v):
            for fn in prog.q["vector"]:
                fn(v)

        @block.scalar
        def _(s):
            for fn in prog.q["scalar"]:
                fn(s)

        @block.tensor
        def _(t):
            for fn in prog.q["tensor"]:
                fn(t)

    return nc


_NC_CACHE = {}


def _get_nc(stage=3):
    if stage not in _NC_CACHE:
        _NC_CACHE[stage] = build_nc(stage)
    return _NC_CACHE[stage]


def _hmat():
    idx = np.arange(128)
    pc = np.zeros((128, 128), np.int64)
    for b in range(7):
        pc += ((idx[:, None] >> b) & 1) * ((idx[None, :] >> b) & 1)
    return np.where(pc % 2 == 0, 1.0, -1.0).astype(ml_dtypes.bfloat16)


def kernel(L0_attn_hidden_states, L0_attn_W_in, L0_attn_conv_weight,
           L0_attn_conv_bias, L0_attn_dt_bias, L0_attn_A_log, L0_attn_D,
           L0_attn_norm_weight, L0_attn_W_out, L0_ffn_SU_ug, L0_ffn_SU_dp,
           L0_ffn_Wscale_ug, L0_ffn_Wscale_dp, L0_ffn_W_up, L0_ffn_W_gate,
           L0_ffn_W_down, _stage=3):
    nc = _get_nc(_stage)
    common = {
        "W_in": np.ascontiguousarray(L0_attn_W_in, np.float32),
        "conv_weight": np.ascontiguousarray(L0_attn_conv_weight, np.float32),
        "conv_bias": np.ascontiguousarray(L0_attn_conv_bias, np.float32),
        "dt_bias": np.ascontiguousarray(L0_attn_dt_bias, np.float32),
        "A_log": np.ascontiguousarray(L0_attn_A_log, np.float32),
        "D_vec": np.ascontiguousarray(L0_attn_D, np.float32),
        "norm_weight": np.ascontiguousarray(L0_attn_norm_weight, np.float32),
        "W_out": np.ascontiguousarray(L0_attn_W_out, np.float32),
        "SU_ug": np.ascontiguousarray(L0_ffn_SU_ug, np.float32),
        "SU_dp": np.ascontiguousarray(L0_ffn_SU_dp, np.float32),
        "Wscale_ug": np.ascontiguousarray(L0_ffn_Wscale_ug, np.float32),
        "Wscale_dp": np.ascontiguousarray(L0_ffn_Wscale_dp, np.float32),
        "W_up": np.ascontiguousarray(L0_ffn_W_up, np.float32),
        "W_gate": np.ascontiguousarray(L0_ffn_W_gate, np.float32),
        "W_down": np.ascontiguousarray(L0_ffn_W_down, np.float32),
        "Hconst": _hmat(),
    }
    hs = np.asarray(L0_attn_hidden_states, np.float32)
    in_maps = []
    for core in range(8):
        m = dict(common)
        m["x_dmaj"] = np.ascontiguousarray(hs[core].T)
        in_maps.append(m)
    res = run_bass_kernel_spmd(nc, in_maps, core_ids=list(range(8)))
    outs = [np.ascontiguousarray(r["out_dmaj"].T) for r in res.results]
    return np.stack(outs, axis=0)


if __name__ == "__main__":
    import sys
    nc = build_nc(int(sys.argv[1]) if len(sys.argv) > 1 else 3)
    print("built ok; instructions:", nc.n_instructions())
